# version 06
# baseline (speedup 1.0000x reference)
import contextlib
import numpy as np
import ml_dtypes
import concourse.bass as bass
import concourse.mybir as mybir
from concourse.bass_utils import run_bass_kernel_spmd

F32 = mybir.dt.float32
BF16 = mybir.dt.bfloat16
AF = mybir.ActivationFunctionType
ALU = mybir.AluOpType
AX = mybir.AxisListType.X

D = 2048
NKC = 16
T = 1024
NH = 16
NEXP = 16
NVE = 18
FF = 1024
NFC = 8
EPS = 1e-5
ISQH = float(1.0 / np.sqrt(128.0))
NBIG = -1e30

ENGINES = ("sync", "tensor", "vector", "scalar", "gpsimd")

PHASES = 3
KV_DEBUG = False


class Plan:
    def __init__(self, nc, stack):
        self.nc = nc
        self.stack = stack
        self.ops = {e: [] for e in ENGINES}
        self.sems = {}
        self.cnt = {}
        self._redir = None

    def sem(self, name):
        self.sems[name] = self.stack.enter_context(self.nc.semaphore(name))
        self.cnt[name] = 0

    def op(self, eng, fn, waits=(), inc=None, amt=1):
        if inc is not None:
            self.cnt[inc] += amt
        entry = ("op", tuple(waits), fn, inc, amt)
        tgt = self._redir[eng] if self._redir is not None else self.ops[eng]
        tgt.append(entry)
        return self.cnt[inc] if inc else None

    def switch(self, n, engines, body_fn):
        assert self._redir is None
        base = dict(self.cnt)
        per_engine_bodies = {e: [] for e in engines}
        finals = []
        for b in range(n):
            self.cnt = dict(base)
            self._redir = {e: [] for e in engines}
            body_fn(b)
            for e in engines:
                per_engine_bodies[e].append(self._redir[e])
            self._redir = None
            finals.append(dict(self.cnt))
        for f in finals[1:]:
            assert f == finals[0], f"switch bodies unequal: {finals[0]} vs {f}"
        self.cnt = finals[0]
        for e in engines:
            self.ops[e].append(("switch", n, per_engine_bodies[e]))

    def barrier(self, extra=(), skip=()):
        waits = [(s, self.cnt[s]) for s in ("pe", "dve", "act")] + list(extra)
        for e in ENGINES:
            if e in skip:
                continue
            self.ops[e].append(("op", tuple(waits), None, None, 0))

    def emit(self, block):
        plan = self

        def replay(eng, entries):
            for ent in entries:
                if ent[0] == "op":
                    _, waits, fn, inc, amt = ent
                    for s, v in waits:
                        if v > 0:
                            eng.wait_ge(plan.sems[s], v)
                    if fn is not None:
                        ins = fn(eng)
                        if inc is not None:
                            ins.then_inc(plan.sems[inc], amt)
                else:
                    _, n, bodies = ent
                    cp = eng.partition_id() % n
                    for b in eng.Switch(cp, n):
                        replay(eng, bodies[b])

        def make(name):
            return lambda eng: replay(eng, plan.ops[name])

        block.sync(make("sync"))
        block.tensor(make("tensor"))
        block.vector(make("vector"))
        block.scalar(make("scalar"))
        block.gpsimd(make("gpsimd"))


def build_nc():
    nc = bass.Bass("TRN2", target_bir_lowering=False)

    dxin = nc.dram_tensor("xin", [T, D], F32, kind="ExternalInput")
    dWqkv = nc.dram_tensor("Wqkv", [D, 3 * D], F32, kind="ExternalInput")
    dbqkvT = nc.dram_tensor("bqkvT", [128, 48], F32, kind="ExternalInput")
    dbqkvV = nc.dram_tensor("bqkvV", [1, D], F32, kind="ExternalInput")
    dWo = nc.dram_tensor("Wo", [D, D], F32, kind="ExternalInput")
    dboT = nc.dram_tensor("boT", [128, 16], F32, kind="ExternalInput")
    dWrT = nc.dram_tensor("WrT", [D, NEXP], F32, kind="ExternalInput")
    dw1 = nc.dram_tensor("w1", [NEXP, D, FF], F32, kind="ExternalInput")
    dv1 = nc.dram_tensor("v1", [NEXP, D, FF], F32, kind="ExternalInput")
    dw2 = nc.dram_tensor("w2", [NEXP, FF, D], F32, kind="ExternalInput")
    dsgT = nc.dram_tensor("sgT", [D, D], F32, kind="ExternalInput")
    dsuT = nc.dram_tensor("suT", [D, D], F32, kind="ExternalInput")
    dsdT = nc.dram_tensor("sdT", [D, D], F32, kind="ExternalInput")
    didf = nc.dram_tensor("identf", [128, 128], F32, kind="ExternalInput")
    dones_colb = nc.dram_tensor("ones_colb", [128, 1], BF16, kind="ExternalInput")
    dones_colf = nc.dram_tensor("ones_colf", [128, 1], F32, kind="ExternalInput")
    dones_row = nc.dram_tensor("ones_row", [1, 512], F32, kind="ExternalInput")
    dtrimask = nc.dram_tensor("trimask", [128, 128], BF16, kind="ExternalInput")
    donehot = nc.dram_tensor("onehot", [NEXP, NEXP * 128], BF16, kind="ExternalInput")

    dy = nc.dram_tensor("y", [T, D], F32, kind="ExternalOutput")
    if KV_DEBUG:
        ddbgK = nc.dram_tensor("dbgK", [4 * D, T], BF16, kind="ExternalOutput")
        ddbgV = nc.dram_tensor("dbgV", [4 * T, D], BF16, kind="ExternalOutput")
    dKc = nc.dram_tensor("Kc", [D, T], BF16, kind="Internal")
    dVc = nc.dram_tensor("Vc", [T, D], BF16, kind="Internal")
    dKg = nc.dram_tensor("Kg", [4 * D, T], BF16, kind="Internal")
    dVg = nc.dram_tensor("Vg", [4 * T, D], BF16, kind="Internal")

    st = contextlib.ExitStack()
    sb = lambda name, shape, dt: st.enter_context(nc.sbuf_tensor(name, shape, dt))
    resid = [sb(f"res{c}", [128, T], F32) for c in range(NKC)]
    xhat = [sb(f"xh{c}", [128, T], BF16) for c in range(NKC)]
    ones_row = sb("s_ones_row", [1, 512], F32)
    ones_colb = sb("s_ones_colb", [128, 1], BF16)
    ones_colf = sb("s_ones_colf", [128, 1], F32)
    identf = sb("s_identf", [128, 128], F32)
    trimask = sb("s_trimask", [128, 128], BF16)
    onehot = sb("s_onehot", [NEXP, NEXP * 128], BF16)
    wrT = sb("wrT", [128, NKC * NEXP], BF16)
    bqkvT = sb("s_bqkvT", [128, 48], F32)
    boT = sb("s_boT", [128, 16], F32)
    gsc = sb("gsc", [128, 80], F32)
    gT = sb("gT", [NEXP, T], BF16)
    ghat = sb("ghat", [128, 2 * T], F32)
    tscr = sb("tscr", [128, 1024], F32)
    arena = sb("arena", [128, 45056], BF16)

    pb = [st.enter_context(nc.psum_tensor(f"pb{i}", [128, 512], F32)) for i in range(8)]

    rowA = ghat[0:1, 0:T]
    rowB = ghat[0:1, T:2 * T]

    def A(off, n):
        return arena[:, off:off + n]

    s_w1 = A(0, 16384)
    s_v1 = A(16384, 16384)
    s_h = A(32768, 8192)
    s_w2 = [A(40960, 2048), A(43008, 2048)]
    s_wq = [A(0, 8192), A(8192, 8192)]
    s_Kst = [A(16384 + i * 1024, 1024) for i in range(4)]
    s_Vst = [A(20480 + i * 512, 512) for i in range(4)]
    s_Q = A(22528, 16384)
    s_Qh = lambda h: arena[:, 22528 + h * 1024:22528 + (h + 1) * 1024]
    s_Kc = [A(0, 4096), A(4096, 4096)]
    s_Vc = [A(8192, 4096), A(12288, 4096)]
    s_eS = [A(16384 + i * 512, 512) for i in range(4)]
    s_wo = [A(0, 8192), A(8192, 8192)]
    xstage = A(0, 32768).bitcast(F32)
    s_ost = [A(0, 4096).bitcast(F32), A(4096, 4096).bitcast(F32)]

    plan = Plan(nc, st)
    for s in ("pe", "dve", "act", "ws0", "ws1", "ws2", "w2s0", "w2s1",
              "xs", "kvs", "cc", "fs0", "fs1", "st0", "st1", "os",
              "kt0", "kt1", "kt2", "kt3", "vt0", "vt1", "vt2", "vt3"):
        plan.sem(s)
    P = plan.op

    state = dict(rnd=0, rel={0: [], 1: [], 2: [], 3: []})

    def round_begin():
        bs = state["rnd"] % 4
        waits = list(state["rel"][bs])
        state["rel"][bs] = []
        return bs, waits

    def round_end(release):
        state["rel"][state["rnd"] % 4] = list(release)
        state["rnd"] += 1

    def clear_rounds():
        for i in range(4):
            state["rel"][i] = []

    def dma(eng, fn, inc, waits=(), amt=16):
        return P(eng, fn, waits=waits, inc=inc, amt=amt)

    dma("sync", lambda e: e.dma_start(out=ones_row[:, :], in_=dones_row[:, :]), "xs")
    dma("sync", lambda e: e.dma_start(out=ones_colb[:, :], in_=dones_colb[:, :]), "xs")
    dma("sync", lambda e: e.dma_start(out=ones_colf[:, :], in_=dones_colf[:, :]), "xs")
    dma("sync", lambda e: e.dma_start(out=identf[:, :], in_=didf[:, :]), "xs")
    dma("sync", lambda e: e.dma_start(out=trimask[:, :], in_=dtrimask[:, :]), "xs")
    dma("sync", lambda e: e.dma_start(out=onehot[:, :], in_=donehot[:, :]), "xs")
    dma("sync", lambda e: e.dma_start(out=bqkvT[:, :], in_=dbqkvT[:, :]), "xs")
    dma("gpsimd", lambda e: e.dma_start(
        out=wrT[:, :].rearrange("p (k e) -> p k e", k=NKC),
        in_=dWrT.ap().rearrange("(k p) e -> p k e", p=128)), "kvs")
    kvs_wrt = plan.cnt["kvs"]
    dma("sync", lambda e: e.dma_start(
        out=xstage.rearrange("p (k f) -> p k f", k=8),
        in_=dxin.ap().rearrange("(k p) f -> p k f", p=128)), "xs")
    xs_total = plan.cnt["xs"]

    def wsrc(mat, e, fh):
        c0, c1 = fh * 512, (fh + 1) * 512
        if e < NEXP:
            d = {"w1": dw1, "v1": dv1}[mat]
            return d.ap()[e][:, c0:c1].rearrange("(k p) f -> p k f", p=128)
        dd = {"w1": dsgT, "v1": dsuT}[mat]
        half = e - NEXP
        return dd.ap()[:, half * FF + c0:half * FF + c1].rearrange("(k p) f -> p k f", p=128)

    def w2src(e, j):
        if e < NEXP:
            return dw2.ap()[e][:, j * 256:(j + 1) * 256].rearrange("(k p) c -> p k c", p=128)
        half = e - NEXP
        return dsdT.ap()[half * FF:(half + 1) * FF, j * 256:(j + 1) * 256].rearrange(
            "(k p) c -> p k c", p=128)

    wready = {}
    w2ready = {}
    wsem_idx = [0]

    def issue_w1v1_half(e, fh, waits=()):
        s = f"ws{wsem_idx[0] % 3}"
        wsem_idx[0] += 1
        dst_w1 = s_w1.rearrange("p (k f) -> p k f", k=NKC)[:, :, fh * 512:(fh + 1) * 512]
        dst_v1 = s_v1.rearrange("p (k f) -> p k f", k=NKC)[:, :, fh * 512:(fh + 1) * 512]
        P("gpsimd", lambda eng, e=e, fh=fh, d=dst_w1: eng.dma_start(
            out=d, in_=wsrc("w1", e, fh)),
          waits=list(waits), inc=s, amt=16)
        v = P("gpsimd", lambda eng, e=e, fh=fh, d=dst_v1: eng.dma_start(
            out=d, in_=wsrc("v1", e, fh)),
          inc=s, amt=16)
        wready[(e, fh)] = (s, v)

    def issue_w1v1(e, waits=()):
        issue_w1v1_half(e, 0, waits=waits)
        issue_w1v1_half(e, 1, waits=waits)

    def issue_w2(e, j, waits=()):
        s = f"w2s{j % 2}"
        v = P("gpsimd", lambda eng, e=e, j=j: eng.dma_start(
            out=s_w2[j % 2].rearrange("p (k c) -> p k c", k=NFC), in_=w2src(e, j)),
          waits=list(waits), inc=s, amt=16)
        w2ready[(e, j)] = (s, v)

    first_tr = True
    for tc in range(8):
        for half in range(2):
            bs, waits = round_begin()
            if first_tr:
                waits = waits + [("xs", xs_total)]
                first_tr = False
            pe_end = None
            for j in range(8):
                c = half * 8 + j
                bank = pb[2 * bs + j // 4]
                i = j % 4
                pe_end = P("tensor", lambda e, tc=tc, c=c, bank=bank, i=i:
                           e.transpose(bank[:, i * 128:(i + 1) * 128],
                                       xstage[:, tc * 2048 + c * 128:tc * 2048 + (c + 1) * 128],
                                       identf[:, :]),
                           waits=waits if j == 0 else (), inc="pe")
            a_end = None
            for j in range(8):
                c = half * 8 + j
                bank = pb[2 * bs + j // 4]
                i = j % 4
                a_end = P("scalar", lambda e, tc=tc, c=c, bank=bank, i=i:
                          e.copy(resid[c][:, tc * 128:(tc + 1) * 128],
                                 bank[:, i * 128:(i + 1) * 128]),
                          waits=[("pe", pe_end)] if j == 0 else (), inc="act")
            round_end([("act", a_end)])
    plan.barrier()
    clear_rounds()

    def emit_ln():
        pe_sq = {}
        for c in range(NKC):
            for th in range(2):
                w = [("pe", pe_sq[(c - 1, th)])] if (c - 1, th) in pe_sq else []
                sq = P("scalar", lambda e, c=c, th=th:
                       e.activation(tscr[:, th * 512:(th + 1) * 512],
                                    resid[c][:, th * 512:(th + 1) * 512], AF.Square),
                       waits=w, inc="act")
                P("tensor", lambda e, c=c, th=th:
                  e.matmul(pb[th][0:1, :], ones_colf[:, 0:1],
                           resid[c][:, th * 512:(th + 1) * 512],
                           start=(c == 0), stop=(c == NKC - 1)))
                pe_sq[(c, th)] = P("tensor", lambda e, c=c, th=th:
                                   e.matmul(pb[2 + th][0:1, :], ones_colf[:, 0:1],
                                            tscr[:, th * 512:(th + 1) * 512],
                                            start=(c == 0), stop=(c == NKC - 1)),
                                   waits=[("act", sq)], inc="pe")
        pe_stats = plan.cnt["pe"]
        for th in range(2):
            a1 = P("scalar", lambda e, th=th:
                   e.activation(rowA[0:1, th * 512:(th + 1) * 512], pb[th][0:1, :],
                                AF.Copy, scale=1.0 / D),
                   waits=[("pe", pe_stats)], inc="act")
            d1 = P("vector", lambda e, th=th:
                   e.tensor_tensor(tscr[0:1, 0:512], rowA[0:1, th * 512:(th + 1) * 512],
                                   rowA[0:1, th * 512:(th + 1) * 512], ALU.mult),
                   waits=[("act", a1)], inc="dve")
            d2 = P("vector", lambda e, th=th:
                   e.scalar_tensor_tensor(tscr[32:33, 0:512], pb[2 + th][0:1, :],
                                          1.0 / D, tscr[0:1, 0:512], ALU.mult, ALU.subtract),
                   waits=[("dve", d1)], inc="dve")
            d3 = P("vector", lambda e:
                   e.tensor_scalar_add(tscr[0:1, 0:512], tscr[32:33, 0:512], EPS),
                   waits=[("dve", d2)], inc="dve")
            a2 = P("scalar", lambda e:
                   e.activation(tscr[32:33, 0:512], tscr[0:1, 0:512], AF.Sqrt),
                   waits=[("dve", d3)], inc="act")
            P("vector", lambda e, th=th:
              e.reciprocal(rowB[0:1, th * 512:(th + 1) * 512], tscr[32:33, 0:512]),
              waits=[("act", a2)], inc="dve")
        dve_stats = plan.cnt["dve"]
        pe_bc = None
        for th in range(2):
            P("tensor", lambda e, th=th:
              e.matmul(pb[4 + th][:, :], ones_row[0:1, 0:128],
                       rowA[0:1, th * 512:(th + 1) * 512], start=True, stop=True),
              waits=[("dve", dve_stats)] if th == 0 else (), inc="pe")
            pe_bc = P("tensor", lambda e, th=th:
                      e.matmul(pb[6 + th][:, :], ones_row[0:1, 0:128],
                               rowB[0:1, th * 512:(th + 1) * 512], start=True, stop=True),
                      inc="pe")
        prevm = 0
        xhat_ready = [0] * NKC
        for c in range(NKC):
            for th in range(2):
                sw = [("pe", pe_bc)] if c == 0 and th == 0 else []
                if prevm:
                    sw = sw + [("dve", prevm)]
                dsub = P("vector", lambda e, c=c, th=th:
                         e.tensor_tensor(tscr[:, 0:512], resid[c][:, th * 512:(th + 1) * 512],
                                         pb[4 + th][:, :], ALU.subtract),
                         waits=sw, inc="dve")
                prevm = P("vector", lambda e, c=c, th=th:
                          e.tensor_tensor(xhat[c][:, th * 512:(th + 1) * 512],
                                          tscr[:, 0:512], pb[6 + th][:, :], ALU.mult),
                          waits=[("dve", dsub)], inc="dve")
            xhat_ready[c] = prevm
        state["xhat_ready"] = xhat_ready
        plan.barrier(skip=("tensor",))
        clear_rounds()
        state["rel"][2] = [("dve", prevm)]
        state["rel"][3] = [("dve", prevm)]

    def emit_router():
        xr = state.get("xhat_ready", [0] * NKC)
        for ts in range(8):
            bank = pb[ts % 2]
            waits = [("kvs", kvs_wrt)] if ts == 0 else []
            if ts >= 2:
                waits = [("act", plan.cnt["act"])]
            pe_s = None
            for k in range(NKC):
                kw = list(waits) if k == 0 else []
                if ts == 0 and xr[k]:
                    kw = kw + [("dve", xr[k])]
                pe_s = P("tensor", lambda e, ts=ts, k=k, bank=bank:
                         e.matmul(bank[:, 0:16], xhat[k][:, ts * 128:(ts + 1) * 128],
                                  wrT[:, k * 16:(k + 1) * 16],
                                  start=(k == 0), stop=(k == NKC - 1)),
                         waits=kw, inc="pe")
            a0 = P("scalar", lambda e, bank=bank:
                   e.activation(gsc[:, 16:32], bank[:, 0:16], AF.Copy),
                   waits=[("pe", pe_s)], inc="act")
            d = P("vector", lambda e:
                  e.reduce_max(gsc[:, 64:65], gsc[:, 16:32], axis=AX, negate=True),
                  waits=[("act", a0)], inc="dve")
            a1 = P("scalar", lambda e:
                   e.activation(gsc[:, 0:16], gsc[:, 16:32], AF.Exp, bias=gsc[:, 64:65]),
                   waits=[("dve", d)], inc="act")
            cur, nxt = (0, 16)
            dd = P("vector", lambda e:
                   e.tensor_copy(gsc[:, 16:32], gsc[:, 0:16]),
                   waits=[("act", a1)], inc="dve")
            cur = 16
            nxt = 32
            for it in range(3):
                m = P("vector", lambda e, cur=cur:
                      e.reduce_max(gsc[:, 65:66], gsc[:, cur:cur + 16], axis=AX),
                      waits=[("dve", dd)], inc="dve")
                eq = P("vector", lambda e, cur=cur:
                       e.tensor_scalar(gsc[:, 48:64], gsc[:, cur:cur + 16],
                                       gsc[:, 65:66], None, ALU.is_ge),
                       waits=[("dve", m)], inc="dve")
                dd = P("vector", lambda e, cur=cur, nxt=nxt:
                       e.scalar_tensor_tensor(gsc[:, nxt:nxt + 16], gsc[:, 48:64],
                                              NBIG, gsc[:, cur:cur + 16], ALU.mult, ALU.add),
                       waits=[("dve", eq)], inc="dve")
                cur, nxt = nxt, cur
            tau = P("vector", lambda e, cur=cur:
                    e.reduce_max(gsc[:, 66:67], gsc[:, cur:cur + 16], axis=AX),
                    waits=[("dve", dd)], inc="dve")
            keep = P("vector", lambda e:
                     e.tensor_scalar(gsc[:, 48:64], gsc[:, 0:16], gsc[:, 66:67],
                                     None, ALU.is_ge),
                     waits=[("dve", tau)], inc="dve")
            gat = P("vector", lambda e:
                    e.tensor_tensor(gsc[:, 16:32], gsc[:, 0:16], gsc[:, 48:64], ALU.mult),
                    waits=[("dve", keep)], inc="dve")
            z = P("vector", lambda e:
                  e.reduce_sum(gsc[:, 67:68], gsc[:, 16:32], axis=AX),
                  waits=[("dve", gat)], inc="dve")
            r = P("vector", lambda e:
                  e.reciprocal(gsc[:, 68:69], gsc[:, 67:68]),
                  waits=[("dve", z)], inc="dve")
            gg = P("vector", lambda e:
                   e.tensor_scalar(gsc[:, 32:48], gsc[:, 16:32], gsc[:, 68:69],
                                   0.5, ALU.mult, ALU.mult),
                   waits=[("dve", r)], inc="dve")
            tp = P("tensor", lambda e, ts=ts, bank=pb[2 + ts % 2]:
                   e.transpose(bank[0:16, 0:128], gsc[:, 32:48], identf[:, :]),
                   waits=[("dve", gg)], inc="pe")
            P("scalar", lambda e, ts=ts, bank=pb[2 + ts % 2]:
              e.activation(gT[0:16, ts * 128:(ts + 1) * 128], bank[0:16, 0:128], AF.Copy),
              waits=[("pe", tp)], inc="act")
        plan.barrier()
        clear_rounds()

    def emit_expert(e):
        routed = e < NEXP
        if routed:
            bs, waits = round_begin()
            ge = []
            for th in range(2):
                m = P("tensor", lambda eng, e=e, th=th, bs=bs:
                      eng.matmul(pb[2 * bs + th][:, :], onehot[:, e * 128:(e + 1) * 128],
                                 gT[:, th * 512:(th + 1) * 512], start=True, stop=True),
                      waits=waits if th == 0 else (), inc="pe")
                g = P("scalar", lambda eng, th=th, bs=bs:
                      eng.activation(ghat[:, th * 512:(th + 1) * 512],
                                     pb[2 * bs + th][:, :], AF.Copy),
                      waits=[("pe", m)], inc="act")
                ge.append(g)
            round_end([("act", ge[-1])])
            ghat_ready = ge[-1]
        else:
            ghat_ready = None

        a_pe_end = None
        epi_dve_end = None
        half_pe_end = None
        scr_war = {0: 0, 1: 0}
        for f in range(NFC):
            for th in range(2):
                bs, waits = round_begin()
                if f == 0 and th == 0:
                    waits = waits + [wready[(e, 0)]]
                if f == 4 and th == 0:
                    waits = waits + [wready[(e, 1)]]
                ta = pb[2 * bs]
                tb = pb[2 * bs + 1]
                pe_end = None
                xr = state.get("xhat_ready", [0] * NKC) if (e == 0 and f == 0 and th == 0) else None
                for k in range(NKC):
                    kw = list(waits) if k == 0 else []
                    if xr is not None and xr[k]:
                        kw = kw + [("dve", xr[k])]
                    for mat, t in ((s_w1, ta), (s_v1, tb)):
                        pe_end = P("tensor", lambda eng, k=k, f=f, mat=mat, th=th, t=t:
                                   eng.matmul(t[:, :], mat[:, k * FF + f * 128:k * FF + (f + 1) * 128],
                                              xhat[k][:, th * 512:(th + 1) * 512],
                                              start=(k == 0), stop=(k == NKC - 1)),
                                   waits=kw if t is ta else (), inc="pe")
                a_pe_end = pe_end
                if f == 3 and th == 1:
                    half_pe_end = pe_end
                scrA = tscr[:, 0:512] if th == 0 else ghat[:, 1024:1536]
                scrB = tscr[:, 512:1024] if th == 0 else ghat[:, 1536:2048]
                sl = P("scalar", lambda eng, t=ta, scrA=scrA:
                       eng.activation(scrA, t[:, :], AF.Silu),
                       waits=[("pe", pe_end), ("dve", scr_war[th])], inc="act")
                if routed:
                    gw = [("act", sl)]
                    if ghat_ready is not None and f == 0 and th == 0:
                        gw.append(("act", ghat_ready))
                    d1 = P("vector", lambda eng, th=th, scrA=scrA, scrB=scrB:
                          eng.tensor_tensor(scrB, scrA,
                                            ghat[:, th * 512:(th + 1) * 512], ALU.mult),
                          waits=gw, inc="dve")
                    d2 = P("vector", lambda eng, f=f, th=th, t=tb, scrB=scrB:
                           eng.tensor_tensor(s_h[:, f * FF + th * 512:f * FF + (th + 1) * 512],
                                             scrB, t[:, :], ALU.mult),
                           waits=[("dve", d1)], inc="dve")
                    scr_war[th] = d2
                else:
                    d2 = P("vector", lambda eng, f=f, th=th, t=tb, scrA=scrA:
                           eng.scalar_tensor_tensor(
                               s_h[:, f * FF + th * 512:f * FF + (th + 1) * 512],
                               scrA, 0.5, t[:, :], ALU.mult, ALU.mult),
                           waits=[("act", sl)], inc="dve")
                    scr_war[th] = d2
                epi_dve_end = d2
                round_end([("dve", d2), ("act", sl)])
        if e + 1 < NVE:
            issue_w1v1_half(e + 1, 0, waits=[("pe", half_pe_end)])
            issue_w1v1_half(e + 1, 1, waits=[("pe", a_pe_end)])
        for j in range(NFC):
            first_of_j = True
            for cc in range(2):
                bs, waits = round_begin()
                if first_of_j:
                    s2, v2 = w2ready[(e, j)]
                    waits = waits + [(s2, v2)]
                    first_of_j = False
                if j == 0 and cc == 0:
                    waits = waits + [("dve", epi_dve_end)]
                tiles = [pb[2 * bs], pb[2 * bs + 1]]
                pe_end = None
                for k in range(NFC):
                    for th in range(2):
                        pe_end = P("tensor", lambda eng, k=k, j=j, cc=cc, th=th, t=tiles[th]:
                                   eng.matmul(t[:, :],
                                              s_w2[j % 2][:, k * 256 + cc * 128:k * 256 + (cc + 1) * 128],
                                              s_h[:, k * FF + th * 512:k * FF + (th + 1) * 512],
                                              start=(k == 0), stop=(k == NFC - 1)),
                                   waits=waits if (k == 0 and th == 0) else (), inc="pe")
                if cc == 1:
                    nj = j + 2
                    ne = e
                    if nj >= NFC:
                        ne, nj = e + 1, nj - NFC
                    if ne < NVE:
                        issue_w2(ne, nj, waits=[("pe", pe_end)])
                dl = None
                for th in range(2):
                    cg = 2 * j + cc
                    dl = P("vector", lambda eng, cg=cg, th=th, t=tiles[th]:
                           eng.tensor_tensor(resid[cg][:, th * 512:(th + 1) * 512],
                                             resid[cg][:, th * 512:(th + 1) * 512],
                                             t[:, :], ALU.add),
                           waits=[("pe", pe_end)] if th == 0 else (), inc="dve")
                round_end([("dve", dl)])

    def emit_ffn():
        issue_w2(0, 0)
        issue_w2(0, 1)
        for e in range(NVE):
            emit_expert(e)
        plan.barrier()
        clear_rounds()

    qkv_pe_gate = [0]

    def emit_qkv():
        dma("sync", lambda e: e.dma_start(out=ghat[0:1, 0:1024], in_=dbqkvV[0:1, 0:1024]), "fs0")
        dma("sync", lambda e: e.dma_start(out=ghat[0:1, 1024:2048], in_=dbqkvV[0:1, 1024:2048]), "fs0")
        fs0_bias = plan.cnt["fs0"]

        wq_pe_end = {}

        def issue_wq(g, waits=()):
            s = f"ws{wsem_idx[0] % 3}"
            wsem_idx[0] += 1
            v = P("gpsimd", lambda eng, g=g: eng.dma_start(
                out=s_wq[g % 2].rearrange("p (k c) -> p k c", k=NKC),
                in_=dWqkv.ap()[:, g * 512:(g + 1) * 512].rearrange("(k p) c -> p k c", p=128)),
                waits=list(waits), inc=s, amt=16)
            return (s, v)

        xr = state.get("xhat_ready", [0] * NKC)
        gseq = [4, 5, 6, 7, 8, 9, 10, 11, 0, 1, 2, 3]
        wqr = {gseq[0]: issue_wq(gseq[0], waits=[("pe", qkv_pe_gate[0])]),
               gseq[1]: issue_wq(gseq[1])}
        kst = [0]
        vst = [0]
        kslot_store = {}
        vslot_store = {}
        for gi, g in enumerate(gseq):
            swq, vwq = wqr[g]
            if g < 8:
                for f in range(4):
                    bs, waits = round_begin()
                    waits = waits + ([(swq, vwq)] if f == 0 else [])
                    pe_end = None
                    tiles = [pb[2 * bs], pb[2 * bs + 1]]
                    for k in range(NKC):
                        kw = list(waits) if (k == 0) else []
                        if gi == 0 and f == 0 and xr[k]:
                            kw = kw + [("dve", xr[k])]
                        for th in range(2):
                            pe_end = P("tensor", lambda eng, g=g, k=k, f=f, th=th, t=tiles[th]:
                                       eng.matmul(t[:, :],
                                                  s_wq[g % 2][:, k * 512 + f * 128:k * 512 + (f + 1) * 128],
                                                  xhat[k][:, th * 512:(th + 1) * 512],
                                                  start=(k == 0), stop=(k == NKC - 1)),
                                       waits=kw if th == 0 else (), inc="pe")
                    if f == 3 and gi + 2 < 12:
                        wqr[gseq[gi + 2]] = issue_wq(gseq[gi + 2], waits=[("pe", pe_end)])
                    al = None
                    fk = g * 4 + f
                    ks = kst[0] % 4
                    for th in range(2):
                        if fk < 16:
                            dst = s_Qh(fk)
                            al = P("vector", lambda eng, dst=dst, th=th, fk=fk, t=tiles[th]:
                                   eng.tensor_scalar(dst[:, th * 512:(th + 1) * 512], t[:, :],
                                                     bqkvT[:, fk:fk + 1], None, ALU.add),
                                   waits=[("pe", pe_end)] if th == 0 else (), inc="dve")
                        else:
                            kw = [("pe", pe_end)] if th == 0 else []
                            if th == 0 and ks in kslot_store:
                                kw = kw + [(f"kt{ks}", kslot_store[ks])]
                            al = P("vector", lambda eng, ks=ks, th=th, fk=fk, t=tiles[th]:
                                   eng.tensor_scalar(s_Kst[ks][:, th * 512:(th + 1) * 512], t[:, :],
                                                     bqkvT[:, fk:fk + 1], None, ALU.add),
                                   waits=kw, inc="dve")
                            if th == 1:
                                fc = fk - 16
                                kslot_store[ks] = dma("sync", lambda eng, ks=ks, fc=fc: eng.dma_start(
                                    out=dKc.ap()[fc * 128:(fc + 1) * 128, :],
                                    in_=s_Kst[ks][:, :]),
                                    f"kt{ks}", waits=[("dve", al)])
                                kst[0] += 1
                    round_end([("dve", al)])
            else:
                hg = g - 8
                bs, waits = round_begin()
                brow = rowA if hg < 2 else rowB
                boff = (hg % 2) * 512
                pe_b = P("tensor", lambda e, t=pb[2 * bs], brow=brow, boff=boff:
                         e.matmul(t[:, :], ones_row[0:1, 0:128],
                                  brow[0:1, boff:boff + 512], start=True, stop=True),
                         waits=waits + [("fs0", fs0_bias), ("dve", plan.cnt["dve"])], inc="pe")
                a_b = P("vector", lambda e, t=pb[2 * bs]:
                        e.tensor_copy(tscr[:, 512:1024], t[:, :]),
                        waits=[("pe", pe_b)], inc="dve")
                round_end([("dve", a_b)])
                vbias_ready = a_b
                for r in range(4):
                    bs, waits = round_begin()
                    waits = waits + ([(swq, vwq)] if r == 0 else [])
                    tiles = [pb[2 * bs], pb[2 * bs + 1]]
                    pe_end = None
                    for i in range(2):
                        ts = 2 * r + i
                        for k in range(NKC):
                            pe_end = P("tensor", lambda eng, g=g, k=k, ts=ts, t=tiles[i]:
                                       eng.matmul(t[:, :],
                                                  xhat[k][:, ts * 128:(ts + 1) * 128],
                                                  s_wq[g % 2][:, k * 512:(k + 1) * 512],
                                                  start=(k == 0), stop=(k == NKC - 1)),
                                       waits=waits if (k == 0 and i == 0) else (), inc="pe")
                    if r == 3 and gi + 2 < 12:
                        wqr[gseq[gi + 2]] = issue_wq(gseq[gi + 2], waits=[("pe", pe_end)])
                    al = None
                    for i in range(2):
                        ts = 2 * r + i
                        vs = vst[0] % 4
                        vw = [("pe", pe_end)] if i == 0 else []
                        if i == 0 and r == 0:
                            vw = vw + [("dve", vbias_ready)]
                        if vs in vslot_store:
                            vw = vw + [(f"vt{vs}", vslot_store[vs])]
                        al = P("vector", lambda eng, vs=vs, t=tiles[i]:
                               eng.tensor_tensor(s_Vst[vs][:, :], t[:, :],
                                                 tscr[:, 512:1024], ALU.add),
                               waits=vw, inc="dve")
                        vslot_store[vs] = dma("sync", lambda eng, vs=vs, ts=ts, hg=hg: eng.dma_start(
                            out=dVc.ap()[ts * 128:(ts + 1) * 128, hg * 512:(hg + 1) * 512],
                            in_=s_Vst[vs][:, :]),
                            f"vt{vs}", waits=[("dve", al)])
                        vst[0] += 1
                    round_end([("dve", al)])
            if g == 11:
                P("gpsimd", lambda e: e.collective_compute(
                    "AllGather", ALU.bypass, replica_groups=[[0, 1, 2, 3], [4, 5, 6, 7]],
                    ins=[dVc.ap().opt()], outs=[dVg.ap().opt()]),
                  waits=[(f"vt{i}", plan.cnt[f"vt{i}"]) for i in range(4)], inc="cc", amt=1)
            if g == 7:
                P("gpsimd", lambda e: e.collective_compute(
                    "AllGather", ALU.bypass, replica_groups=[[0, 1, 2, 3], [4, 5, 6, 7]],
                    ins=[dKc.ap().opt()], outs=[dKg.ap().opt()]),
                  waits=[(f"kt{i}", plan.cnt[f"kt{i}"]) for i in range(4)], inc="cc", amt=1)
        plan.barrier()
        clear_rounds()
        if KV_DEBUG:
            P("sync", lambda e: e.dma_start(out=ddbgK.ap(), in_=dKg.ap()),
              waits=[("cc", plan.cnt["cc"])], inc="os", amt=16)
            P("sync", lambda e: e.dma_start(out=ddbgV.ap(), in_=dVg.ap()),
              inc="os", amt=16)
            plan.ops["sync"].append(("op", (("os", plan.cnt["os"]),), None, None, 0))

    def emit_flash():
        cc_k = plan.cnt["cc"] - 1
        cc_v = plan.cnt["cc"]

        def body(cp):
            qlo, qhi = cp, 7 - cp
            pe_head_end = {}

            def issue_kv(h, extra_waits):
                s = f"fs{h % 2}"
                first = (h < 2)
                for kb in range(8):
                    owner = kb if kb < 4 else 7 - kb
                    slot = 0 if kb < 4 else 1
                    w = list(extra_waits) if kb == 0 else []
                    if kb == 0 and first:
                        w = w + [("cc", cc_k)]
                    P("sync", lambda eng, h=h, kb=kb, owner=owner, slot=slot:
                      eng.dma_start(
                          out=s_Kc[h % 2][:, kb * 512:(kb + 1) * 512],
                          in_=dKg.ap()[owner * D + h * 128:owner * D + (h + 1) * 128,
                                       slot * 512:(slot + 1) * 512]),
                      waits=w, inc=s, amt=16)
                last = None
                for kb in range(8):
                    owner = kb if kb < 4 else 7 - kb
                    slot = 0 if kb < 4 else 1
                    w = [("cc", cc_v)] if (kb == 0 and first) else []
                    last = P("sync", lambda eng, h=h, kb=kb, owner=owner, slot=slot:
                             eng.dma_start(
                                 out=s_Vc[h % 2][:, kb * 4 * 128:(kb + 1) * 4 * 128]
                                 .rearrange("p (s d) -> p s d", s=4),
                                 in_=dVg.ap()[owner * T + slot * 512:owner * T + slot * 512 + 512,
                                              h * 128:(h + 1) * 128]
                                 .rearrange("(s p) d -> p s d", p=128)),
                             waits=w, inc=s, amt=16)
                return last

            sbank = [pb[0], pb[1], pb[3], pb[7]]
            fsv = {}
            fsv[0] = issue_kv(0, [])
            fsv[1] = issue_kv(1, [])
            act_eS = {}
            pe_eS = {}
            prev_norm_dve = {0: 0, 1: 0}
            prev_l_act = 0
            prev_rb_act = 0
            for h in range(NH):
                fwait = (f"fs{h % 2}", fsv[h])
                first_of_head = True
                for si, qb in enumerate((qlo, qhi)):
                    kts = 4 * (qb + 1)
                    avu = pb[4 + si]
                    lps = pb[2]

                    def emit_sc(kt, fw):
                        qrel = max(0, kt * 128 - qb * 512)
                        w = 512 - qrel
                        sw = list(fw)
                        if kt % 4 in act_eS:
                            sw.append(("act", act_eS[kt % 4]))
                        return P("tensor", lambda eng, h=h, kt=kt, si=si, qrel=qrel, w=w, sb=sbank:
                                 eng.matmul(sb[kt % 4][:, 0:w],
                                            s_Kc[h % 2][:, kt * 128:(kt + 1) * 128],
                                            s_Qh(h)[:, si * 512 + qrel:si * 512 + 512],
                                            start=True, stop=True),
                                 waits=sw, inc="pe")

                    sc_of = {}
                    for kt in range(4):
                        if kt < kts:
                            fw = [fwait] if (kt == 0 and first_of_head) else []
                            sc_of[kt] = emit_sc(kt, fw)
                    first_of_head = False
                    for kt in range(kts):
                        qrel = max(0, kt * 128 - qb * 512)
                        w = 512 - qrel
                        diag = kt >= qb * 4
                        es = s_eS[kt % 4]
                        ew = [("pe", sc_of[kt])]
                        if kt % 4 in pe_eS:
                            ew.append(("pe", max(sc_of[kt], pe_eS[kt % 4])))
                        ae = P("scalar", lambda eng, kt=kt, w=w, es=es, sb=sbank:
                               eng.activation(es[:, 0:w], sb[kt % 4][:, 0:w],
                                              AF.Exp, scale=ISQH),
                               waits=ew, inc="act")
                        act_eS[kt % 4] = ae
                        if diag:
                            md = P("vector", lambda eng, es=es:
                                   eng.tensor_tensor(es[:, 0:128], es[:, 0:128],
                                                     trimask[:, :], ALU.mult),
                                   waits=[("act", ae)], inc="dve")
                            mmw = [("dve", md)]
                        else:
                            mmw = [("act", ae)]
                        if kt == 0:
                            mmw = mmw + [("dve", prev_norm_dve[si])]
                        P("tensor", lambda eng, h=h, kt=kt, qrel=qrel, w=w, es=es, avu=avu, kts=kts:
                          eng.matmul(avu[:, qrel:512],
                                     s_Vc[h % 2][:, kt * 128:(kt + 1) * 128],
                                     es[:, 0:w], start=(kt == 0), stop=(kt == kts - 1)),
                          waits=mmw, inc="pe")
                        lw = [("dve", prev_l_act)] if kt == 0 else []
                        lm = P("tensor", lambda eng, kt=kt, qrel=qrel, w=w, es=es, lps=lps, kts=kts:
                               eng.matmul(lps[0:1, qrel:512], ones_colb[:, 0:1],
                                          es[:, 0:w], start=(kt == 0), stop=(kt == kts - 1)),
                               waits=lw, inc="pe")
                        pe_eS[kt % 4] = lm
                        if kt + 4 < kts:
                            sc_of[kt + 4] = emit_sc(kt + 4, [])
                    la = P("vector", lambda eng, lps=lps:
                           eng.tensor_copy(tscr[0:1, 0:512], lps[0:1, 0:512]),
                           waits=[("pe", plan.cnt["pe"])], inc="dve")
                    prev_l_act = la
                    rc = P("vector", lambda eng:
                           eng.reciprocal(tscr[0:1, 512:1024], tscr[0:1, 0:512]),
                           waits=[("dve", la)], inc="dve")
                    rb = P("tensor", lambda eng:
                           eng.matmul(pb[6][:, :], ones_row[0:1, 0:128],
                                      tscr[0:1, 512:1024], start=True, stop=True),
                           waits=[("dve", rc), ("dve", prev_rb_act)], inc="pe")
                    ra = P("vector", lambda eng:
                           eng.tensor_copy(tscr[:, 512:1024], pb[6][:, :]),
                           waits=[("pe", rb)], inc="dve")
                    prev_rb_act = ra
                    nd = P("vector", lambda eng, h=h, si=si, avu=avu:
                           eng.tensor_tensor(xhat[h][:, si * 512:(si + 1) * 512],
                                             avu[:, :], tscr[:, 512:1024], ALU.mult),
                           waits=[("dve", ra)], inc="dve")
                    prev_norm_dve[si] = nd
                pe_head_end[h] = plan.cnt["pe"]
                if h + 2 < NH:
                    fsv[h + 2] = issue_kv(h + 2, [("pe", pe_head_end[h])])

        plan.switch(4, ["sync", "tensor", "vector", "scalar"], body)
        plan.barrier()
        clear_rounds()

    def emit_wo():
        dma("sync", lambda e: e.dma_start(out=boT[:, :], in_=dboT[:, :]), "fs0")
        fs0_bo = plan.cnt["fs0"]

        wor = {}

        def issue_wo(cg, waits=()):
            s = f"ws{wsem_idx[0] % 3}"
            wsem_idx[0] += 1
            v = P("gpsimd", lambda eng, cg=cg: eng.dma_start(
                out=s_wo[cg % 2].rearrange("p (k c) -> p k c", k=NKC),
                in_=dWo.ap()[:, cg * 512:(cg + 1) * 512].rearrange("(k p) c -> p k c", p=128)),
                waits=list(waits), inc=s, amt=16)
            return (s, v)

        pe_now = plan.cnt["pe"]
        wor[0] = issue_wo(0, waits=[("pe", pe_now)])
        wor[1] = issue_wo(1)
        for cg in range(4):
            swo, vwo = wor[cg]
            for cc in range(4):
                bs, waits = round_begin()
                if cc == 0:
                    waits = waits + [(swo, vwo), ("fs0", fs0_bo)]
                tiles = [pb[2 * bs], pb[2 * bs + 1]]
                pe_end = None
                for k in range(NKC):
                    for th in range(2):
                        pe_end = P("tensor", lambda eng, cg=cg, k=k, cc=cc, th=th, t=tiles[th]:
                                   eng.matmul(t[:, :],
                                              s_wo[cg % 2][:, k * 512 + cc * 128:k * 512 + (cc + 1) * 128],
                                              xhat[k][:, th * 512:(th + 1) * 512],
                                              start=(k == 0), stop=(k == NKC - 1)),
                                   waits=waits if (k == 0 and th == 0) else (), inc="pe")
                if cc == 3 and cg + 2 < 4:
                    wor[cg + 2] = issue_wo(cg + 2, waits=[("pe", pe_end)])
                dl = None
                cglob = cg * 4 + cc
                for th in range(2):
                    dl = P("vector", lambda eng, cglob=cglob, th=th, t=tiles[th]:
                           eng.scalar_tensor_tensor(resid[cglob][:, th * 512:(th + 1) * 512],
                                                    t[:, :], boT[:, cglob:cglob + 1],
                                                    resid[cglob][:, th * 512:(th + 1) * 512],
                                                    ALU.add, ALU.add),
                           waits=[("pe", pe_end)] if th == 0 else (), inc="dve")
                round_end([("dve", dl)])
        plan.barrier()
        clear_rounds()

    def emit_out():
        for tc in range(8):
            a_end = None
            for half in range(2):
                bs, waits = round_begin()
                pe_end = None
                for j in range(8):
                    c = half * 8 + j
                    bank = pb[2 * bs + j // 4]
                    i = j % 4
                    pe_end = P("tensor", lambda e, tc=tc, c=c, bank=bank, i=i:
                               e.transpose(bank[:, i * 128:(i + 1) * 128],
                                           resid[c][:, tc * 128:(tc + 1) * 128],
                                           identf[:, :]),
                               waits=waits if j == 0 else (), inc="pe")
                for j in range(8):
                    c = half * 8 + j
                    bank = pb[2 * bs + j // 4]
                    i = j % 4
                    aw = [("pe", pe_end)] if j == 0 else []
                    if half == 0 and j == 0:
                        aw = aw + [(f"st{tc % 2}", plan.cnt[f"st{tc % 2}"])]
                    a_end = P("scalar", lambda e, tc=tc, c=c, bank=bank, i=i:
                              e.copy(s_ost[tc % 2][:, c * 128:(c + 1) * 128],
                                     bank[:, i * 128:(i + 1) * 128]),
                              waits=aw, inc="act")
                round_end([("act", a_end)])
            dma("sync", lambda e, tc=tc: e.dma_start(
                out=dy.ap()[tc * 128:(tc + 1) * 128, :], in_=s_ost[tc % 2][:, :]),
                f"st{tc % 2}", waits=[("act", a_end)])
        fin = (("st0", plan.cnt["st0"]), ("st1", plan.cnt["st1"]))
        for e in ENGINES:
            plan.ops[e].append(("op", fin, None, None, 0))

    issue_w1v1(0, waits=[("pe", plan.cnt["pe"])])
    emit_ln()
    emit_router()
    emit_ffn()
    if PHASES >= 2:
        qkv_pe_gate[0] = plan.cnt["pe"]
        emit_ln()
        emit_qkv()
        emit_flash()
        emit_wo()
    if PHASES >= 3:
        issue_w1v1(0, waits=[("pe", plan.cnt["pe"])])
        emit_ln()
        emit_router()
        emit_ffn()
    emit_out()

    with nc.Block() as block, st:
        plan.emit(block)
    return nc


_NC_CACHE = [None]


def _token_slices(core):
    cp = core % 4
    s = core // 4
    return s, (cp * 512, (cp + 1) * 512), ((7 - cp) * 512, (8 - cp) * 512)


def kernel(L0_attn_x, L0_attn_Wqkv, L0_attn_bqkv, L0_attn_Wo, L0_attn_bo,
           L0_ffn_router_w, L0_ffn_w1, L0_ffn_v1, L0_ffn_w2,
           L0_ffn_sg_w, L0_ffn_su_w, L0_ffn_sd_w):
    f32 = np.float32
    x = np.ascontiguousarray(np.asarray(L0_attn_x, dtype=f32))
    Wqkv = np.ascontiguousarray(np.asarray(L0_attn_Wqkv, dtype=f32))
    bqkv = np.ascontiguousarray(np.asarray(L0_attn_bqkv, dtype=f32))
    Wo = np.ascontiguousarray(np.asarray(L0_attn_Wo, dtype=f32))
    bo = np.ascontiguousarray(np.asarray(L0_attn_bo, dtype=f32))
    router_w = np.ascontiguousarray(np.asarray(L0_ffn_router_w, dtype=f32))
    w1 = np.ascontiguousarray(np.asarray(L0_ffn_w1, dtype=f32))
    v1 = np.ascontiguousarray(np.asarray(L0_ffn_v1, dtype=f32))
    w2 = np.ascontiguousarray(np.asarray(L0_ffn_w2, dtype=f32))
    sgT = np.ascontiguousarray(np.asarray(L0_ffn_sg_w, dtype=f32).T)
    suT = np.ascontiguousarray(np.asarray(L0_ffn_su_w, dtype=f32).T)
    sdT = np.ascontiguousarray(np.asarray(L0_ffn_sd_w, dtype=f32).T)
    WrT = np.ascontiguousarray(router_w.T)
    bqkvT = np.ascontiguousarray(bqkv.reshape(48, 128).T)
    bqkvV = np.ascontiguousarray(bqkv[2 * D:3 * D].reshape(1, D))
    boT_np = np.ascontiguousarray(bo.reshape(16, 128).T)

    identf = np.eye(128, dtype=f32)
    ones_colb = np.ones((128, 1), dtype=ml_dtypes.bfloat16)
    ones_colf = np.ones((128, 1), dtype=f32)
    ones_row = np.ones((1, 512), dtype=f32)
    tri = np.zeros((128, 128), dtype=ml_dtypes.bfloat16)
    kk, qq = np.meshgrid(np.arange(128), np.arange(128), indexing="ij")
    tri[kk <= qq] = 1.0
    onehot = np.zeros((NEXP, NEXP * 128), dtype=ml_dtypes.bfloat16)
    for e in range(NEXP):
        onehot[e, e * 128:(e + 1) * 128] = 1.0

    shared = dict(Wqkv=Wqkv, bqkvT=bqkvT, bqkvV=bqkvV, Wo=Wo, boT=boT_np, WrT=WrT,
                  w1=w1, v1=v1, w2=w2, sgT=sgT, suT=suT, sdT=sdT,
                  identf=identf, ones_colb=ones_colb, ones_colf=ones_colf,
                  ones_row=ones_row, trimask=tri, onehot=onehot)
    in_maps = []
    for core in range(8):
        s, (a0, a1), (b0, b1) = _token_slices(core)
        xc = np.ascontiguousarray(np.concatenate([x[s, a0:a1], x[s, b0:b1]], axis=0))
        m = dict(shared)
        m["xin"] = xc
        in_maps.append(m)

    if _NC_CACHE[0] is None:
        _NC_CACHE[0] = build_nc()
    nc = _NC_CACHE[0]
    res = run_bass_kernel_spmd(nc, in_maps, core_ids=list(range(8)))

    out = np.empty((2, 4096, D), dtype=f32)
    for core in range(8):
        s, (a0, a1), (b0, b1) = _token_slices(core)
        yc = np.asarray(res.results[core]["y"])
        out[s, a0:a1] = yc[0:512]
        out[s, b0:b1] = yc[512:1024]
    if KV_DEBUG:
        global DBG_RESULTS
        DBG_RESULTS = res.results
    return out


# version 07
# speedup vs baseline: 1.0001x; 1.0001x over previous
import contextlib
import numpy as np
import ml_dtypes
import concourse.bass as bass
import concourse.mybir as mybir
from concourse.bass_utils import run_bass_kernel_spmd

F32 = mybir.dt.float32
BF16 = mybir.dt.bfloat16
AF = mybir.ActivationFunctionType
ALU = mybir.AluOpType
AX = mybir.AxisListType.X

D = 2048
NKC = 16
T = 1024
NH = 16
NEXP = 16
NVE = 18
FF = 1024
NFC = 8
EPS = 1e-5
ISQH = float(1.0 / np.sqrt(128.0))
NBIG = -1e30

ENGINES = ("sync", "tensor", "vector", "scalar", "gpsimd")

PHASES = 3
KV_DEBUG = False


class Plan:
    def __init__(self, nc, stack):
        self.nc = nc
        self.stack = stack
        self.ops = {e: [] for e in ENGINES}
        self.sems = {}
        self.cnt = {}
        self._redir = None

    def sem(self, name):
        self.sems[name] = self.stack.enter_context(self.nc.semaphore(name))
        self.cnt[name] = 0

    def op(self, eng, fn, waits=(), inc=None, amt=1):
        if inc is not None:
            self.cnt[inc] += amt
        entry = ["op", tuple(waits), fn, inc, amt, self.cnt[inc] if inc else 0]
        tgt = self._redir[eng] if self._redir is not None else self.ops[eng]
        tgt.append(entry)
        return self.cnt[inc] if inc else None

    def switch(self, n, engines, body_fn):
        assert self._redir is None
        base = dict(self.cnt)
        per_engine_bodies = {e: [] for e in engines}
        finals = []
        for b in range(n):
            self.cnt = dict(base)
            self._redir = {e: [] for e in engines}
            body_fn(b)
            for e in engines:
                per_engine_bodies[e].append(self._redir[e])
            self._redir = None
            finals.append(dict(self.cnt))
        for f in finals[1:]:
            assert f == finals[0], f"switch bodies unequal: {finals[0]} vs {f}"
        self.cnt = finals[0]
        for e in engines:
            self.ops[e].append(("switch", n, per_engine_bodies[e]))

    def barrier(self, extra=(), skip=()):
        waits = [(s, self.cnt[s]) for s in ("pe", "dve", "act")] + list(extra)
        for e in ENGINES:
            if e in skip:
                continue
            self.ops[e].append(["op", tuple(waits), None, None, 0, 0])

    def compress(self):
        import bisect
        csems = ("pe", "dve", "act")
        waited = {c: set() for c in csems}

        def scan(entries):
            for ent in entries:
                if ent[0] == "op":
                    for sname, v in ent[1]:
                        if sname in waited:
                            waited[sname].add(v)
                else:
                    for body in ent[2]:
                        scan(body)
        for e in ENGINES:
            scan(self.ops[e])
        keep = {c: sorted(waited[c] | {self.cnt[c]}) for c in csems}

        def rewrite(entries):
            for ent in entries:
                if ent[0] == "op":
                    ent[1] = tuple((sn, bisect.bisect_right(keep[sn], v) if sn in waited else v)
                                   for sn, v in ent[1])
                    if ent[3] in waited and ent[5] not in (set(keep[ent[3]]) if False else keep_sets[ent[3]]):
                        ent[3] = None
                else:
                    for body in ent[2]:
                        rewrite(body)
        keep_sets = {c: set(keep[c]) for c in csems}
        for e in ENGINES:
            rewrite(self.ops[e])
        for c in csems:
            self.cnt[c] = len(keep[c])

    def emit(self, block):
        plan = self

        def replay(eng, entries):
            for ent in entries:
                if ent[0] == "op":
                    _, waits, fn, inc, amt = ent[0], ent[1], ent[2], ent[3], ent[4]
                    for s, v in waits:
                        if v > 0:
                            eng.wait_ge(plan.sems[s], v)
                    if fn is not None:
                        ins = fn(eng)
                        if inc is not None:
                            ins.then_inc(plan.sems[inc], amt)
                else:
                    _, n, bodies = ent
                    cp = eng.partition_id() % n
                    for b in eng.Switch(cp, n):
                        replay(eng, bodies[b])

        def make(name):
            return lambda eng: replay(eng, plan.ops[name])

        block.sync(make("sync"))
        block.tensor(make("tensor"))
        block.vector(make("vector"))
        block.scalar(make("scalar"))
        block.gpsimd(make("gpsimd"))


def build_nc():
    nc = bass.Bass("TRN2", target_bir_lowering=False)

    dxin = nc.dram_tensor("xin", [T, D], F32, kind="ExternalInput")
    dWqkv = nc.dram_tensor("Wqkv", [D, 3 * D], F32, kind="ExternalInput")
    dbqkvT = nc.dram_tensor("bqkvT", [128, 48], F32, kind="ExternalInput")
    dbqkvV = nc.dram_tensor("bqkvV", [1, D], F32, kind="ExternalInput")
    dWo = nc.dram_tensor("Wo", [D, D], F32, kind="ExternalInput")
    dboT = nc.dram_tensor("boT", [128, 16], F32, kind="ExternalInput")
    dWrT = nc.dram_tensor("WrT", [D, NEXP], F32, kind="ExternalInput")
    dw1 = nc.dram_tensor("w1", [NEXP, D, FF], F32, kind="ExternalInput")
    dv1 = nc.dram_tensor("v1", [NEXP, D, FF], F32, kind="ExternalInput")
    dw2 = nc.dram_tensor("w2", [NEXP, FF, D], F32, kind="ExternalInput")
    dsgT = nc.dram_tensor("sgT", [D, D], F32, kind="ExternalInput")
    dsuT = nc.dram_tensor("suT", [D, D], F32, kind="ExternalInput")
    dsdT = nc.dram_tensor("sdT", [D, D], F32, kind="ExternalInput")
    didf = nc.dram_tensor("identf", [128, 128], F32, kind="ExternalInput")
    dones_colb = nc.dram_tensor("ones_colb", [128, 1], BF16, kind="ExternalInput")
    dones_colf = nc.dram_tensor("ones_colf", [128, 1], F32, kind="ExternalInput")
    dones_row = nc.dram_tensor("ones_row", [1, 512], F32, kind="ExternalInput")
    dtrimask = nc.dram_tensor("trimask", [128, 128], BF16, kind="ExternalInput")
    donehot = nc.dram_tensor("onehot", [NEXP, NEXP * 128], BF16, kind="ExternalInput")

    dy = nc.dram_tensor("y", [T, D], F32, kind="ExternalOutput")
    if KV_DEBUG:
        ddbgK = nc.dram_tensor("dbgK", [4 * D, T], BF16, kind="ExternalOutput")
        ddbgV = nc.dram_tensor("dbgV", [4 * T, D], BF16, kind="ExternalOutput")
    dKc = nc.dram_tensor("Kc", [D, T], BF16, kind="Internal")
    dVc = nc.dram_tensor("Vc", [T, D], BF16, kind="Internal")
    dKg = nc.dram_tensor("Kg", [4 * D, T], BF16, kind="Internal")
    dVg = nc.dram_tensor("Vg", [4 * T, D], BF16, kind="Internal")

    st = contextlib.ExitStack()
    sb = lambda name, shape, dt: st.enter_context(nc.sbuf_tensor(name, shape, dt))
    resid = [sb(f"res{c}", [128, T], F32) for c in range(NKC)]
    xhat = [sb(f"xh{c}", [128, T], BF16) for c in range(NKC)]
    ones_row = sb("s_ones_row", [1, 512], F32)
    ones_colb = sb("s_ones_colb", [128, 1], BF16)
    ones_colf = sb("s_ones_colf", [128, 1], F32)
    identf = sb("s_identf", [128, 128], F32)
    trimask = sb("s_trimask", [128, 128], BF16)
    onehot = sb("s_onehot", [NEXP, NEXP * 128], BF16)
    wrT = sb("wrT", [128, NKC * NEXP], BF16)
    bqkvT = sb("s_bqkvT", [128, 48], F32)
    boT = sb("s_boT", [128, 16], F32)
    gsc = sb("gsc", [128, 80], F32)
    gT = sb("gT", [NEXP, T], BF16)
    ghat = sb("ghat", [128, 2 * T], F32)
    tscr = sb("tscr", [128, 1024], F32)
    arena = sb("arena", [128, 45056], BF16)

    pb = [st.enter_context(nc.psum_tensor(f"pb{i}", [128, 512], F32)) for i in range(8)]

    rowA = ghat[0:1, 0:T]
    rowB = ghat[0:1, T:2 * T]

    def A(off, n):
        return arena[:, off:off + n]

    s_w1 = A(0, 16384)
    s_v1 = A(16384, 16384)
    s_h = A(32768, 8192)
    s_w2 = [A(40960, 2048), A(43008, 2048)]
    s_wq = [A(0, 8192), A(8192, 8192)]
    s_Kst = [A(16384 + i * 1024, 1024) for i in range(4)]
    s_Vst = [A(20480 + i * 512, 512) for i in range(4)]
    s_Q = A(22528, 16384)
    s_Qh = lambda h: arena[:, 22528 + h * 1024:22528 + (h + 1) * 1024]
    s_Kc = [A(0, 4096), A(4096, 4096)]
    s_Vc = [A(8192, 4096), A(12288, 4096)]
    s_eS = [A(16384 + i * 512, 512) for i in range(4)]
    s_wo = [A(0, 8192), A(8192, 8192)]
    xstage = A(0, 32768).bitcast(F32)
    s_ost = [A(0, 4096).bitcast(F32), A(4096, 4096).bitcast(F32)]

    plan = Plan(nc, st)
    for s in ("pe", "dve", "act", "ws0", "ws1", "ws2", "w2s0", "w2s1",
              "xs", "kvs", "cc", "fs0", "fs1", "st0", "st1", "os",
              "kt0", "kt1", "kt2", "kt3", "vt0", "vt1", "vt2", "vt3"):
        plan.sem(s)
    P = plan.op

    state = dict(rnd=0, rel={0: [], 1: [], 2: [], 3: []})

    def round_begin():
        bs = state["rnd"] % 4
        waits = list(state["rel"][bs])
        state["rel"][bs] = []
        return bs, waits

    def round_end(release):
        state["rel"][state["rnd"] % 4] = list(release)
        state["rnd"] += 1

    def clear_rounds():
        for i in range(4):
            state["rel"][i] = []

    def dma(eng, fn, inc, waits=(), amt=16):
        return P(eng, fn, waits=waits, inc=inc, amt=amt)

    dma("sync", lambda e: e.dma_start(out=ones_row[:, :], in_=dones_row[:, :]), "xs")
    dma("sync", lambda e: e.dma_start(out=ones_colb[:, :], in_=dones_colb[:, :]), "xs")
    dma("sync", lambda e: e.dma_start(out=ones_colf[:, :], in_=dones_colf[:, :]), "xs")
    dma("sync", lambda e: e.dma_start(out=identf[:, :], in_=didf[:, :]), "xs")
    dma("sync", lambda e: e.dma_start(out=trimask[:, :], in_=dtrimask[:, :]), "xs")
    dma("sync", lambda e: e.dma_start(out=onehot[:, :], in_=donehot[:, :]), "xs")
    dma("sync", lambda e: e.dma_start(out=bqkvT[:, :], in_=dbqkvT[:, :]), "xs")
    dma("gpsimd", lambda e: e.dma_start(
        out=wrT[:, :].rearrange("p (k e) -> p k e", k=NKC),
        in_=dWrT.ap().rearrange("(k p) e -> p k e", p=128)), "kvs")
    kvs_wrt = plan.cnt["kvs"]
    dma("sync", lambda e: e.dma_start(
        out=xstage[:, 0:8192].rearrange("p (k f) -> p k f", k=4),
        in_=dxin.ap()[0:512, :].rearrange("(k p) f -> p k f", p=128)), "xs")
    xs_half = plan.cnt["xs"]
    dma("sync", lambda e: e.dma_start(
        out=xstage[:, 8192:16384].rearrange("p (k f) -> p k f", k=4),
        in_=dxin.ap()[512:1024, :].rearrange("(k p) f -> p k f", p=128)), "xs")
    xs_total = plan.cnt["xs"]

    def wsrc(mat, e, fh):
        c0, c1 = fh * 512, (fh + 1) * 512
        if e < NEXP:
            d = {"w1": dw1, "v1": dv1}[mat]
            return d.ap()[e][:, c0:c1].rearrange("(k p) f -> p k f", p=128)
        dd = {"w1": dsgT, "v1": dsuT}[mat]
        half = e - NEXP
        return dd.ap()[:, half * FF + c0:half * FF + c1].rearrange("(k p) f -> p k f", p=128)

    def w2src(e, j):
        if e < NEXP:
            return dw2.ap()[e][:, j * 256:(j + 1) * 256].rearrange("(k p) c -> p k c", p=128)
        half = e - NEXP
        return dsdT.ap()[half * FF:(half + 1) * FF, j * 256:(j + 1) * 256].rearrange(
            "(k p) c -> p k c", p=128)

    wready = {}
    w2ready = {}
    wsem_idx = [0]

    def issue_w1v1_half(e, fh, waits=()):
        s = f"ws{wsem_idx[0] % 3}"
        wsem_idx[0] += 1
        dst_w1 = s_w1.rearrange("p (k f) -> p k f", k=NKC)[:, :, fh * 512:(fh + 1) * 512]
        dst_v1 = s_v1.rearrange("p (k f) -> p k f", k=NKC)[:, :, fh * 512:(fh + 1) * 512]
        P("gpsimd", lambda eng, e=e, fh=fh, d=dst_w1: eng.dma_start(
            out=d, in_=wsrc("w1", e, fh)),
          waits=list(waits), inc=s, amt=16)
        v = P("gpsimd", lambda eng, e=e, fh=fh, d=dst_v1: eng.dma_start(
            out=d, in_=wsrc("v1", e, fh)),
          inc=s, amt=16)
        wready[(e, fh)] = (s, v)

    def issue_w1v1(e, waits=()):
        issue_w1v1_half(e, 0, waits=waits)
        issue_w1v1_half(e, 1, waits=waits)

    def issue_w2(e, j, waits=()):
        s = f"w2s{j % 2}"
        v = P("gpsimd", lambda eng, e=e, j=j: eng.dma_start(
            out=s_w2[j % 2].rearrange("p (k c) -> p k c", k=NFC), in_=w2src(e, j)),
          waits=list(waits), inc=s, amt=16)
        w2ready[(e, j)] = (s, v)

    gated = set()
    for tc in range(8):
        for half in range(2):
            bs, waits = round_begin()
            gate = xs_half if tc < 4 else xs_total
            if gate not in gated:
                waits = waits + [("xs", gate)]
                gated.add(gate)
            pe_end = None
            for j in range(8):
                c = half * 8 + j
                bank = pb[2 * bs + j // 4]
                i = j % 4
                pe_end = P("tensor", lambda e, tc=tc, c=c, bank=bank, i=i:
                           e.transpose(bank[:, i * 128:(i + 1) * 128],
                                       xstage[:, tc * 2048 + c * 128:tc * 2048 + (c + 1) * 128],
                                       identf[:, :]),
                           waits=waits if j == 0 else (), inc="pe")
            a_end = None
            for j in range(8):
                c = half * 8 + j
                bank = pb[2 * bs + j // 4]
                i = j % 4
                a_end = P("scalar", lambda e, tc=tc, c=c, bank=bank, i=i:
                          e.copy(resid[c][:, tc * 128:(tc + 1) * 128],
                                 bank[:, i * 128:(i + 1) * 128]),
                          waits=[("pe", pe_end)] if j == 0 else (), inc="act")
            round_end([("act", a_end)])
    plan.barrier()
    clear_rounds()

    def emit_ln():
        pe_sq = {}
        for c in range(NKC):
            for th in range(2):
                w = [("pe", pe_sq[(c - 1, th)])] if (c - 1, th) in pe_sq else []
                sq = P("scalar", lambda e, c=c, th=th:
                       e.activation(tscr[:, th * 512:(th + 1) * 512],
                                    resid[c][:, th * 512:(th + 1) * 512], AF.Square),
                       waits=w, inc="act")
                P("tensor", lambda e, c=c, th=th:
                  e.matmul(pb[th][0:1, :], ones_colf[:, 0:1],
                           resid[c][:, th * 512:(th + 1) * 512],
                           start=(c == 0), stop=(c == NKC - 1)))
                pe_sq[(c, th)] = P("tensor", lambda e, c=c, th=th:
                                   e.matmul(pb[2 + th][0:1, :], ones_colf[:, 0:1],
                                            tscr[:, th * 512:(th + 1) * 512],
                                            start=(c == 0), stop=(c == NKC - 1)),
                                   waits=[("act", sq)], inc="pe")
        pe_stats = plan.cnt["pe"]
        for th in range(2):
            a1 = P("scalar", lambda e, th=th:
                   e.activation(rowA[0:1, th * 512:(th + 1) * 512], pb[th][0:1, :],
                                AF.Copy, scale=1.0 / D),
                   waits=[("pe", pe_stats)], inc="act")
            d1 = P("vector", lambda e, th=th:
                   e.tensor_tensor(tscr[0:1, 0:512], rowA[0:1, th * 512:(th + 1) * 512],
                                   rowA[0:1, th * 512:(th + 1) * 512], ALU.mult),
                   waits=[("act", a1)], inc="dve")
            d2 = P("vector", lambda e, th=th:
                   e.scalar_tensor_tensor(tscr[32:33, 0:512], pb[2 + th][0:1, :],
                                          1.0 / D, tscr[0:1, 0:512], ALU.mult, ALU.subtract),
                   waits=[("dve", d1)], inc="dve")
            d3 = P("vector", lambda e:
                   e.tensor_scalar_add(tscr[0:1, 0:512], tscr[32:33, 0:512], EPS),
                   waits=[("dve", d2)], inc="dve")
            a2 = P("scalar", lambda e:
                   e.activation(tscr[32:33, 0:512], tscr[0:1, 0:512], AF.Sqrt),
                   waits=[("dve", d3)], inc="act")
            P("vector", lambda e, th=th:
              e.reciprocal(rowB[0:1, th * 512:(th + 1) * 512], tscr[32:33, 0:512]),
              waits=[("act", a2)], inc="dve")
        dve_stats = plan.cnt["dve"]
        pe_bc = None
        for th in range(2):
            P("tensor", lambda e, th=th:
              e.matmul(pb[4 + th][:, :], ones_row[0:1, 0:128],
                       rowA[0:1, th * 512:(th + 1) * 512], start=True, stop=True),
              waits=[("dve", dve_stats)] if th == 0 else (), inc="pe")
            pe_bc = P("tensor", lambda e, th=th:
                      e.matmul(pb[6 + th][:, :], ones_row[0:1, 0:128],
                               rowB[0:1, th * 512:(th + 1) * 512], start=True, stop=True),
                      inc="pe")
        prevm = 0
        xhat_ready = [0] * NKC
        for c in range(NKC):
            for th in range(2):
                sw = [("pe", pe_bc)] if c == 0 and th == 0 else []
                if prevm:
                    sw = sw + [("dve", prevm)]
                dsub = P("vector", lambda e, c=c, th=th:
                         e.tensor_tensor(tscr[:, 0:512], resid[c][:, th * 512:(th + 1) * 512],
                                         pb[4 + th][:, :], ALU.subtract),
                         waits=sw, inc="dve")
                prevm = P("vector", lambda e, c=c, th=th:
                          e.tensor_tensor(xhat[c][:, th * 512:(th + 1) * 512],
                                          tscr[:, 0:512], pb[6 + th][:, :], ALU.mult),
                          waits=[("dve", dsub)], inc="dve")
            xhat_ready[c] = prevm
        state["xhat_ready"] = xhat_ready
        plan.barrier(skip=("tensor",))
        clear_rounds()
        state["rel"][2] = [("dve", prevm)]
        state["rel"][3] = [("dve", prevm)]

    def emit_router():
        xr = state.get("xhat_ready", [0] * NKC)
        for ts in range(8):
            bank = pb[ts % 2]
            waits = [("kvs", kvs_wrt)] if ts == 0 else []
            if ts >= 2:
                waits = [("act", plan.cnt["act"])]
            pe_s = None
            for k in range(NKC):
                kw = list(waits) if k == 0 else []
                if ts == 0 and xr[k]:
                    kw = kw + [("dve", xr[k])]
                pe_s = P("tensor", lambda e, ts=ts, k=k, bank=bank:
                         e.matmul(bank[:, 0:16], xhat[k][:, ts * 128:(ts + 1) * 128],
                                  wrT[:, k * 16:(k + 1) * 16],
                                  start=(k == 0), stop=(k == NKC - 1)),
                         waits=kw, inc="pe")
            a0 = P("scalar", lambda e, bank=bank:
                   e.activation(gsc[:, 16:32], bank[:, 0:16], AF.Copy),
                   waits=[("pe", pe_s)], inc="act")
            d = P("vector", lambda e:
                  e.reduce_max(gsc[:, 64:65], gsc[:, 16:32], axis=AX, negate=True),
                  waits=[("act", a0)], inc="dve")
            a1 = P("scalar", lambda e:
                   e.activation(gsc[:, 0:16], gsc[:, 16:32], AF.Exp, bias=gsc[:, 64:65]),
                   waits=[("dve", d)], inc="act")
            cur, nxt = (0, 16)
            dd = P("vector", lambda e:
                   e.tensor_copy(gsc[:, 16:32], gsc[:, 0:16]),
                   waits=[("act", a1)], inc="dve")
            cur = 16
            nxt = 32
            for it in range(3):
                m = P("vector", lambda e, cur=cur:
                      e.reduce_max(gsc[:, 65:66], gsc[:, cur:cur + 16], axis=AX),
                      waits=[("dve", dd)], inc="dve")
                eq = P("vector", lambda e, cur=cur:
                       e.tensor_scalar(gsc[:, 48:64], gsc[:, cur:cur + 16],
                                       gsc[:, 65:66], None, ALU.is_ge),
                       waits=[("dve", m)], inc="dve")
                dd = P("vector", lambda e, cur=cur, nxt=nxt:
                       e.scalar_tensor_tensor(gsc[:, nxt:nxt + 16], gsc[:, 48:64],
                                              NBIG, gsc[:, cur:cur + 16], ALU.mult, ALU.add),
                       waits=[("dve", eq)], inc="dve")
                cur, nxt = nxt, cur
            tau = P("vector", lambda e, cur=cur:
                    e.reduce_max(gsc[:, 66:67], gsc[:, cur:cur + 16], axis=AX),
                    waits=[("dve", dd)], inc="dve")
            keep = P("vector", lambda e:
                     e.tensor_scalar(gsc[:, 48:64], gsc[:, 0:16], gsc[:, 66:67],
                                     None, ALU.is_ge),
                     waits=[("dve", tau)], inc="dve")
            gat = P("vector", lambda e:
                    e.tensor_tensor(gsc[:, 16:32], gsc[:, 0:16], gsc[:, 48:64], ALU.mult),
                    waits=[("dve", keep)], inc="dve")
            z = P("vector", lambda e:
                  e.reduce_sum(gsc[:, 67:68], gsc[:, 16:32], axis=AX),
                  waits=[("dve", gat)], inc="dve")
            r = P("vector", lambda e:
                  e.reciprocal(gsc[:, 68:69], gsc[:, 67:68]),
                  waits=[("dve", z)], inc="dve")
            gg = P("vector", lambda e:
                   e.tensor_scalar(gsc[:, 32:48], gsc[:, 16:32], gsc[:, 68:69],
                                   0.5, ALU.mult, ALU.mult),
                   waits=[("dve", r)], inc="dve")
            tp = P("tensor", lambda e, ts=ts, bank=pb[2 + ts % 2]:
                   e.transpose(bank[0:16, 0:128], gsc[:, 32:48], identf[:, :]),
                   waits=[("dve", gg)], inc="pe")
            P("scalar", lambda e, ts=ts, bank=pb[2 + ts % 2]:
              e.activation(gT[0:16, ts * 128:(ts + 1) * 128], bank[0:16, 0:128], AF.Copy),
              waits=[("pe", tp)], inc="act")
        plan.barrier()
        clear_rounds()

    def emit_expert(e):
        routed = e < NEXP
        if routed:
            bs, waits = round_begin()
            ge = []
            for th in range(2):
                m = P("tensor", lambda eng, e=e, th=th, bs=bs:
                      eng.matmul(pb[2 * bs + th][:, :], onehot[:, e * 128:(e + 1) * 128],
                                 gT[:, th * 512:(th + 1) * 512], start=True, stop=True),
                      waits=waits if th == 0 else (), inc="pe")
                g = P("scalar", lambda eng, th=th, bs=bs:
                      eng.activation(ghat[:, th * 512:(th + 1) * 512],
                                     pb[2 * bs + th][:, :], AF.Copy),
                      waits=[("pe", m)], inc="act")
                ge.append(g)
            round_end([("act", ge[-1])])
            ghat_ready = ge[-1]
        else:
            ghat_ready = None

        a_pe_end = None
        epi_dve_end = None
        half_pe_end = None
        scr_war = {0: 0, 1: 0}
        for f in range(NFC):
            for th in range(2):
                bs, waits = round_begin()
                if f == 0 and th == 0:
                    waits = waits + [wready[(e, 0)]]
                if f == 4 and th == 0:
                    waits = waits + [wready[(e, 1)]]
                ta = pb[2 * bs]
                tb = pb[2 * bs + 1]
                pe_end = None
                xr = state.get("xhat_ready", [0] * NKC) if (e == 0 and f == 0 and th == 0) else None
                for k in range(NKC):
                    kw = list(waits) if k == 0 else []
                    if xr is not None and xr[k]:
                        kw = kw + [("dve", xr[k])]
                    for mat, t in ((s_w1, ta), (s_v1, tb)):
                        pe_end = P("tensor", lambda eng, k=k, f=f, mat=mat, th=th, t=t:
                                   eng.matmul(t[:, :], mat[:, k * FF + f * 128:k * FF + (f + 1) * 128],
                                              xhat[k][:, th * 512:(th + 1) * 512],
                                              start=(k == 0), stop=(k == NKC - 1)),
                                   waits=kw if t is ta else (), inc="pe")
                a_pe_end = pe_end
                if f == 3 and th == 1:
                    half_pe_end = pe_end
                scrA = tscr[:, 0:512] if th == 0 else ghat[:, 1024:1536]
                scrB = tscr[:, 512:1024] if th == 0 else ghat[:, 1536:2048]
                sl = P("scalar", lambda eng, t=ta, scrA=scrA:
                       eng.activation(scrA, t[:, :], AF.Silu),
                       waits=[("pe", pe_end), ("dve", scr_war[th])], inc="act")
                if routed:
                    gw = [("act", sl)]
                    if ghat_ready is not None and f == 0 and th == 0:
                        gw.append(("act", ghat_ready))
                    d1 = P("vector", lambda eng, th=th, scrA=scrA, scrB=scrB:
                          eng.tensor_tensor(scrB, scrA,
                                            ghat[:, th * 512:(th + 1) * 512], ALU.mult),
                          waits=gw, inc="dve")
                    d2 = P("vector", lambda eng, f=f, th=th, t=tb, scrB=scrB:
                           eng.tensor_tensor(s_h[:, f * FF + th * 512:f * FF + (th + 1) * 512],
                                             scrB, t[:, :], ALU.mult),
                           waits=[("dve", d1)], inc="dve")
                    scr_war[th] = d2
                else:
                    d2 = P("vector", lambda eng, f=f, th=th, t=tb, scrA=scrA:
                           eng.scalar_tensor_tensor(
                               s_h[:, f * FF + th * 512:f * FF + (th + 1) * 512],
                               scrA, 0.5, t[:, :], ALU.mult, ALU.mult),
                           waits=[("act", sl)], inc="dve")
                    scr_war[th] = d2
                epi_dve_end = d2
                round_end([("dve", d2), ("act", sl)])
        if e + 1 < NVE:
            issue_w1v1_half(e + 1, 0, waits=[("pe", half_pe_end)])
            issue_w1v1_half(e + 1, 1, waits=[("pe", a_pe_end)])
        for j in range(NFC):
            first_of_j = True
            for cc in range(2):
                bs, waits = round_begin()
                if first_of_j:
                    s2, v2 = w2ready[(e, j)]
                    waits = waits + [(s2, v2)]
                    first_of_j = False
                if j == 0 and cc == 0:
                    waits = waits + [("dve", epi_dve_end)]
                tiles = [pb[2 * bs], pb[2 * bs + 1]]
                pe_end = None
                for k in range(NFC):
                    for th in range(2):
                        pe_end = P("tensor", lambda eng, k=k, j=j, cc=cc, th=th, t=tiles[th]:
                                   eng.matmul(t[:, :],
                                              s_w2[j % 2][:, k * 256 + cc * 128:k * 256 + (cc + 1) * 128],
                                              s_h[:, k * FF + th * 512:k * FF + (th + 1) * 512],
                                              start=(k == 0), stop=(k == NFC - 1)),
                                   waits=waits if (k == 0 and th == 0) else (), inc="pe")
                if cc == 1:
                    nj = j + 2
                    ne = e
                    if nj >= NFC:
                        ne, nj = e + 1, nj - NFC
                    if ne < NVE:
                        issue_w2(ne, nj, waits=[("pe", pe_end)])
                dl = None
                for th in range(2):
                    cg = 2 * j + cc
                    dl = P("vector", lambda eng, cg=cg, th=th, t=tiles[th]:
                           eng.tensor_tensor(resid[cg][:, th * 512:(th + 1) * 512],
                                             resid[cg][:, th * 512:(th + 1) * 512],
                                             t[:, :], ALU.add),
                           waits=[("pe", pe_end)] if th == 0 else (), inc="dve")
                round_end([("dve", dl)])

    def emit_ffn():
        issue_w2(0, 0)
        issue_w2(0, 1)
        for e in range(NVE):
            emit_expert(e)
        plan.barrier()
        clear_rounds()

    qkv_pe_gate = [0]

    def emit_qkv():
        dma("sync", lambda e: e.dma_start(out=ghat[0:1, 0:1024], in_=dbqkvV[0:1, 0:1024]), "fs0")
        dma("sync", lambda e: e.dma_start(out=ghat[0:1, 1024:2048], in_=dbqkvV[0:1, 1024:2048]), "fs0")
        fs0_bias = plan.cnt["fs0"]

        wq_pe_end = {}

        def issue_wq(g, waits=()):
            s = f"ws{wsem_idx[0] % 3}"
            wsem_idx[0] += 1
            v = P("gpsimd", lambda eng, g=g: eng.dma_start(
                out=s_wq[g % 2].rearrange("p (k c) -> p k c", k=NKC),
                in_=dWqkv.ap()[:, g * 512:(g + 1) * 512].rearrange("(k p) c -> p k c", p=128)),
                waits=list(waits), inc=s, amt=16)
            return (s, v)

        xr = state.get("xhat_ready", [0] * NKC)
        gseq = [4, 5, 6, 7, 8, 9, 10, 11, 0, 1, 2, 3]
        wqr = {gseq[0]: issue_wq(gseq[0], waits=[("pe", qkv_pe_gate[0])]),
               gseq[1]: issue_wq(gseq[1])}
        kst = [0]
        vst = [0]
        kslot_store = {}
        vslot_store = {}
        for gi, g in enumerate(gseq):
            swq, vwq = wqr[g]
            if g < 8:
                for f in range(4):
                    bs, waits = round_begin()
                    waits = waits + ([(swq, vwq)] if f == 0 else [])
                    pe_end = None
                    tiles = [pb[2 * bs], pb[2 * bs + 1]]
                    for k in range(NKC):
                        kw = list(waits) if (k == 0) else []
                        if gi == 0 and f == 0 and xr[k]:
                            kw = kw + [("dve", xr[k])]
                        for th in range(2):
                            pe_end = P("tensor", lambda eng, g=g, k=k, f=f, th=th, t=tiles[th]:
                                       eng.matmul(t[:, :],
                                                  s_wq[g % 2][:, k * 512 + f * 128:k * 512 + (f + 1) * 128],
                                                  xhat[k][:, th * 512:(th + 1) * 512],
                                                  start=(k == 0), stop=(k == NKC - 1)),
                                       waits=kw if th == 0 else (), inc="pe")
                    if f == 3 and gi + 2 < 12:
                        wqr[gseq[gi + 2]] = issue_wq(gseq[gi + 2], waits=[("pe", pe_end)])
                    al = None
                    fk = g * 4 + f
                    ks = kst[0] % 4
                    for th in range(2):
                        if fk < 16:
                            dst = s_Qh(fk)
                            al = P("vector", lambda eng, dst=dst, th=th, fk=fk, t=tiles[th]:
                                   eng.tensor_scalar(dst[:, th * 512:(th + 1) * 512], t[:, :],
                                                     bqkvT[:, fk:fk + 1], None, ALU.add),
                                   waits=[("pe", pe_end)] if th == 0 else (), inc="dve")
                        else:
                            kw = [("pe", pe_end)] if th == 0 else []
                            if th == 0 and ks in kslot_store:
                                kw = kw + [(f"kt{ks}", kslot_store[ks])]
                            al = P("vector", lambda eng, ks=ks, th=th, fk=fk, t=tiles[th]:
                                   eng.tensor_scalar(s_Kst[ks][:, th * 512:(th + 1) * 512], t[:, :],
                                                     bqkvT[:, fk:fk + 1], None, ALU.add),
                                   waits=kw, inc="dve")
                            if th == 1:
                                fc = fk - 16
                                kslot_store[ks] = dma("sync", lambda eng, ks=ks, fc=fc: eng.dma_start(
                                    out=dKc.ap()[fc * 128:(fc + 1) * 128, :],
                                    in_=s_Kst[ks][:, :]),
                                    f"kt{ks}", waits=[("dve", al)])
                                kst[0] += 1
                    round_end([("dve", al)])
            else:
                hg = g - 8
                bs, waits = round_begin()
                brow = rowA if hg < 2 else rowB
                boff = (hg % 2) * 512
                pe_b = P("tensor", lambda e, t=pb[2 * bs], brow=brow, boff=boff:
                         e.matmul(t[:, :], ones_row[0:1, 0:128],
                                  brow[0:1, boff:boff + 512], start=True, stop=True),
                         waits=waits + [("fs0", fs0_bias), ("dve", plan.cnt["dve"])], inc="pe")
                a_b = P("vector", lambda e, t=pb[2 * bs]:
                        e.tensor_copy(tscr[:, 512:1024], t[:, :]),
                        waits=[("pe", pe_b)], inc="dve")
                round_end([("dve", a_b)])
                vbias_ready = a_b
                for r in range(4):
                    bs, waits = round_begin()
                    waits = waits + ([(swq, vwq)] if r == 0 else [])
                    tiles = [pb[2 * bs], pb[2 * bs + 1]]
                    pe_end = None
                    for i in range(2):
                        ts = 2 * r + i
                        for k in range(NKC):
                            pe_end = P("tensor", lambda eng, g=g, k=k, ts=ts, t=tiles[i]:
                                       eng.matmul(t[:, :],
                                                  xhat[k][:, ts * 128:(ts + 1) * 128],
                                                  s_wq[g % 2][:, k * 512:(k + 1) * 512],
                                                  start=(k == 0), stop=(k == NKC - 1)),
                                       waits=waits if (k == 0 and i == 0) else (), inc="pe")
                    if r == 3 and gi + 2 < 12:
                        wqr[gseq[gi + 2]] = issue_wq(gseq[gi + 2], waits=[("pe", pe_end)])
                    al = None
                    for i in range(2):
                        ts = 2 * r + i
                        vs = vst[0] % 4
                        vw = [("pe", pe_end)] if i == 0 else []
                        if i == 0 and r == 0:
                            vw = vw + [("dve", vbias_ready)]
                        if vs in vslot_store:
                            vw = vw + [(f"vt{vs}", vslot_store[vs])]
                        al = P("vector", lambda eng, vs=vs, t=tiles[i]:
                               eng.tensor_tensor(s_Vst[vs][:, :], t[:, :],
                                                 tscr[:, 512:1024], ALU.add),
                               waits=vw, inc="dve")
                        vslot_store[vs] = dma("sync", lambda eng, vs=vs, ts=ts, hg=hg: eng.dma_start(
                            out=dVc.ap()[ts * 128:(ts + 1) * 128, hg * 512:(hg + 1) * 512],
                            in_=s_Vst[vs][:, :]),
                            f"vt{vs}", waits=[("dve", al)])
                        vst[0] += 1
                    round_end([("dve", al)])
            if g == 11:
                P("gpsimd", lambda e: e.collective_compute(
                    "AllGather", ALU.bypass, replica_groups=[[0, 1, 2, 3], [4, 5, 6, 7]],
                    ins=[dVc.ap().opt()], outs=[dVg.ap().opt()]),
                  waits=[(f"vt{i}", plan.cnt[f"vt{i}"]) for i in range(4)], inc="cc", amt=1)
            if g == 7:
                P("gpsimd", lambda e: e.collective_compute(
                    "AllGather", ALU.bypass, replica_groups=[[0, 1, 2, 3], [4, 5, 6, 7]],
                    ins=[dKc.ap().opt()], outs=[dKg.ap().opt()]),
                  waits=[(f"kt{i}", plan.cnt[f"kt{i}"]) for i in range(4)], inc="cc", amt=1)
        plan.barrier()
        clear_rounds()
        if KV_DEBUG:
            P("sync", lambda e: e.dma_start(out=ddbgK.ap(), in_=dKg.ap()),
              waits=[("cc", plan.cnt["cc"])], inc="os", amt=16)
            P("sync", lambda e: e.dma_start(out=ddbgV.ap(), in_=dVg.ap()),
              inc="os", amt=16)
            plan.ops["sync"].append(["op", (("os", plan.cnt["os"]),), None, None, 0, 0])

    def emit_flash():
        cc_k = plan.cnt["cc"] - 1
        cc_v = plan.cnt["cc"]

        def body(cp):
            qlo, qhi = cp, 7 - cp
            pe_head_end = {}

            def issue_kv(h, extra_waits):
                s = f"fs{h % 2}"
                first = (h < 2)
                for kb in range(8):
                    owner = kb if kb < 4 else 7 - kb
                    slot = 0 if kb < 4 else 1
                    w = list(extra_waits) if kb == 0 else []
                    if kb == 0 and first:
                        w = w + [("cc", cc_k)]
                    P("sync", lambda eng, h=h, kb=kb, owner=owner, slot=slot:
                      eng.dma_start(
                          out=s_Kc[h % 2][:, kb * 512:(kb + 1) * 512],
                          in_=dKg.ap()[owner * D + h * 128:owner * D + (h + 1) * 128,
                                       slot * 512:(slot + 1) * 512]),
                      waits=w, inc=s, amt=16)
                last = None
                for kb in range(8):
                    owner = kb if kb < 4 else 7 - kb
                    slot = 0 if kb < 4 else 1
                    w = [("cc", cc_v)] if (kb == 0 and first) else []
                    last = P("sync", lambda eng, h=h, kb=kb, owner=owner, slot=slot:
                             eng.dma_start(
                                 out=s_Vc[h % 2][:, kb * 4 * 128:(kb + 1) * 4 * 128]
                                 .rearrange("p (s d) -> p s d", s=4),
                                 in_=dVg.ap()[owner * T + slot * 512:owner * T + slot * 512 + 512,
                                              h * 128:(h + 1) * 128]
                                 .rearrange("(s p) d -> p s d", p=128)),
                             waits=w, inc=s, amt=16)
                return last

            sbank = [pb[0], pb[1], pb[3], pb[7]]
            fsv = {}
            fsv[0] = issue_kv(0, [])
            fsv[1] = issue_kv(1, [])
            act_eS = {}
            pe_eS = {}
            prev_norm_dve = {0: 0, 1: 0}
            prev_l_act = 0
            prev_rb_act = 0
            for h in range(NH):
                fwait = (f"fs{h % 2}", fsv[h])
                first_of_head = True
                for si, qb in enumerate((qlo, qhi)):
                    kts = 4 * (qb + 1)
                    avu = pb[4 + si]
                    lps = pb[2]

                    def emit_sc(kt, fw):
                        qrel = max(0, kt * 128 - qb * 512)
                        w = 512 - qrel
                        sw = list(fw)
                        if kt % 4 in act_eS:
                            sw.append(("act", act_eS[kt % 4]))
                        return P("tensor", lambda eng, h=h, kt=kt, si=si, qrel=qrel, w=w, sb=sbank:
                                 eng.matmul(sb[kt % 4][:, 0:w],
                                            s_Kc[h % 2][:, kt * 128:(kt + 1) * 128],
                                            s_Qh(h)[:, si * 512 + qrel:si * 512 + 512],
                                            start=True, stop=True),
                                 waits=sw, inc="pe")

                    sc_of = {}
                    for kt in range(4):
                        if kt < kts:
                            fw = [fwait] if (kt == 0 and first_of_head) else []
                            sc_of[kt] = emit_sc(kt, fw)
                    first_of_head = False
                    for kt in range(kts):
                        qrel = max(0, kt * 128 - qb * 512)
                        w = 512 - qrel
                        diag = kt >= qb * 4
                        es = s_eS[kt % 4]
                        ew = [("pe", sc_of[kt])]
                        if kt % 4 in pe_eS:
                            ew.append(("pe", max(sc_of[kt], pe_eS[kt % 4])))
                        ae = P("scalar", lambda eng, kt=kt, w=w, es=es, sb=sbank:
                               eng.activation(es[:, 0:w], sb[kt % 4][:, 0:w],
                                              AF.Exp, scale=ISQH),
                               waits=ew, inc="act")
                        act_eS[kt % 4] = ae
                        if diag:
                            md = P("vector", lambda eng, es=es:
                                   eng.tensor_tensor(es[:, 0:128], es[:, 0:128],
                                                     trimask[:, :], ALU.mult),
                                   waits=[("act", ae)], inc="dve")
                            mmw = [("dve", md)]
                        else:
                            mmw = [("act", ae)]
                        if kt == 0:
                            mmw = mmw + [("dve", prev_norm_dve[si])]
                        P("tensor", lambda eng, h=h, kt=kt, qrel=qrel, w=w, es=es, avu=avu, kts=kts:
                          eng.matmul(avu[:, qrel:512],
                                     s_Vc[h % 2][:, kt * 128:(kt + 1) * 128],
                                     es[:, 0:w], start=(kt == 0), stop=(kt == kts - 1)),
                          waits=mmw, inc="pe")
                        lw = [("dve", prev_l_act)] if kt == 0 else []
                        lm = P("tensor", lambda eng, kt=kt, qrel=qrel, w=w, es=es, lps=lps, kts=kts:
                               eng.matmul(lps[0:1, qrel:512], ones_colb[:, 0:1],
                                          es[:, 0:w], start=(kt == 0), stop=(kt == kts - 1)),
                               waits=lw, inc="pe")
                        pe_eS[kt % 4] = lm
                        if kt + 4 < kts:
                            sc_of[kt + 4] = emit_sc(kt + 4, [])
                    la = P("vector", lambda eng, lps=lps:
                           eng.tensor_copy(tscr[0:1, 0:512], lps[0:1, 0:512]),
                           waits=[("pe", plan.cnt["pe"])], inc="dve")
                    prev_l_act = la
                    rc = P("vector", lambda eng:
                           eng.reciprocal(tscr[0:1, 512:1024], tscr[0:1, 0:512]),
                           waits=[("dve", la)], inc="dve")
                    rb = P("tensor", lambda eng:
                           eng.matmul(pb[6][:, :], ones_row[0:1, 0:128],
                                      tscr[0:1, 512:1024], start=True, stop=True),
                           waits=[("dve", rc), ("dve", prev_rb_act)], inc="pe")
                    ra = P("vector", lambda eng:
                           eng.tensor_copy(tscr[:, 512:1024], pb[6][:, :]),
                           waits=[("pe", rb)], inc="dve")
                    prev_rb_act = ra
                    nd = P("vector", lambda eng, h=h, si=si, avu=avu:
                           eng.tensor_tensor(xhat[h][:, si * 512:(si + 1) * 512],
                                             avu[:, :], tscr[:, 512:1024], ALU.mult),
                           waits=[("dve", ra)], inc="dve")
                    prev_norm_dve[si] = nd
                pe_head_end[h] = plan.cnt["pe"]
                if h + 2 < NH:
                    fsv[h + 2] = issue_kv(h + 2, [("pe", pe_head_end[h])])

        plan.switch(4, ["sync", "tensor", "vector", "scalar"], body)
        plan.barrier()
        clear_rounds()

    def emit_wo():
        dma("sync", lambda e: e.dma_start(out=boT[:, :], in_=dboT[:, :]), "fs0")
        fs0_bo = plan.cnt["fs0"]

        wor = {}

        def issue_wo(cg, waits=()):
            s = f"ws{wsem_idx[0] % 3}"
            wsem_idx[0] += 1
            v = P("gpsimd", lambda eng, cg=cg: eng.dma_start(
                out=s_wo[cg % 2].rearrange("p (k c) -> p k c", k=NKC),
                in_=dWo.ap()[:, cg * 512:(cg + 1) * 512].rearrange("(k p) c -> p k c", p=128)),
                waits=list(waits), inc=s, amt=16)
            return (s, v)

        pe_now = plan.cnt["pe"]
        wor[0] = issue_wo(0, waits=[("pe", pe_now)])
        wor[1] = issue_wo(1)
        for cg in range(4):
            swo, vwo = wor[cg]
            for cc in range(4):
                bs, waits = round_begin()
                if cc == 0:
                    waits = waits + [(swo, vwo), ("fs0", fs0_bo)]
                tiles = [pb[2 * bs], pb[2 * bs + 1]]
                pe_end = None
                for k in range(NKC):
                    for th in range(2):
                        pe_end = P("tensor", lambda eng, cg=cg, k=k, cc=cc, th=th, t=tiles[th]:
                                   eng.matmul(t[:, :],
                                              s_wo[cg % 2][:, k * 512 + cc * 128:k * 512 + (cc + 1) * 128],
                                              xhat[k][:, th * 512:(th + 1) * 512],
                                              start=(k == 0), stop=(k == NKC - 1)),
                                   waits=waits if (k == 0 and th == 0) else (), inc="pe")
                if cc == 3 and cg + 2 < 4:
                    wor[cg + 2] = issue_wo(cg + 2, waits=[("pe", pe_end)])
                dl = None
                cglob = cg * 4 + cc
                for th in range(2):
                    dl = P("vector", lambda eng, cglob=cglob, th=th, t=tiles[th]:
                           eng.scalar_tensor_tensor(resid[cglob][:, th * 512:(th + 1) * 512],
                                                    t[:, :], boT[:, cglob:cglob + 1],
                                                    resid[cglob][:, th * 512:(th + 1) * 512],
                                                    ALU.add, ALU.add),
                           waits=[("pe", pe_end)] if th == 0 else (), inc="dve")
                round_end([("dve", dl)])
        plan.barrier()
        clear_rounds()

    def emit_out():
        for tc in range(8):
            a_end = None
            for half in range(2):
                bs, waits = round_begin()
                pe_end = None
                for j in range(8):
                    c = half * 8 + j
                    bank = pb[2 * bs + j // 4]
                    i = j % 4
                    pe_end = P("tensor", lambda e, tc=tc, c=c, bank=bank, i=i:
                               e.transpose(bank[:, i * 128:(i + 1) * 128],
                                           resid[c][:, tc * 128:(tc + 1) * 128],
                                           identf[:, :]),
                               waits=waits if j == 0 else (), inc="pe")
                for j in range(8):
                    c = half * 8 + j
                    bank = pb[2 * bs + j // 4]
                    i = j % 4
                    aw = [("pe", pe_end)] if j == 0 else []
                    if half == 0 and j == 0:
                        aw = aw + [(f"st{tc % 2}", plan.cnt[f"st{tc % 2}"])]
                    a_end = P("scalar", lambda e, tc=tc, c=c, bank=bank, i=i:
                              e.copy(s_ost[tc % 2][:, c * 128:(c + 1) * 128],
                                     bank[:, i * 128:(i + 1) * 128]),
                              waits=aw, inc="act")
                round_end([("act", a_end)])
            dma("sync", lambda e, tc=tc: e.dma_start(
                out=dy.ap()[tc * 128:(tc + 1) * 128, :], in_=s_ost[tc % 2][:, :]),
                f"st{tc % 2}", waits=[("act", a_end)])
        fin = (("st0", plan.cnt["st0"]), ("st1", plan.cnt["st1"]))
        for e in ENGINES:
            plan.ops[e].append(["op", fin, None, None, 0, 0])

    issue_w1v1(0, waits=[("pe", plan.cnt["pe"])])
    emit_ln()
    emit_router()
    emit_ffn()
    if PHASES >= 2:
        qkv_pe_gate[0] = plan.cnt["pe"]
        emit_ln()
        emit_qkv()
        emit_flash()
        emit_wo()
    if PHASES >= 3:
        issue_w1v1(0, waits=[("pe", plan.cnt["pe"])])
        emit_ln()
        emit_router()
        emit_ffn()
    emit_out()

    with nc.Block() as block, st:
        plan.emit(block)
    return nc


_NC_CACHE = [None]


def _token_slices(core):
    cp = core % 4
    s = core // 4
    return s, (cp * 512, (cp + 1) * 512), ((7 - cp) * 512, (8 - cp) * 512)


def kernel(L0_attn_x, L0_attn_Wqkv, L0_attn_bqkv, L0_attn_Wo, L0_attn_bo,
           L0_ffn_router_w, L0_ffn_w1, L0_ffn_v1, L0_ffn_w2,
           L0_ffn_sg_w, L0_ffn_su_w, L0_ffn_sd_w):
    f32 = np.float32
    x = np.ascontiguousarray(np.asarray(L0_attn_x, dtype=f32))
    Wqkv = np.ascontiguousarray(np.asarray(L0_attn_Wqkv, dtype=f32))
    bqkv = np.ascontiguousarray(np.asarray(L0_attn_bqkv, dtype=f32))
    Wo = np.ascontiguousarray(np.asarray(L0_attn_Wo, dtype=f32))
    bo = np.ascontiguousarray(np.asarray(L0_attn_bo, dtype=f32))
    router_w = np.ascontiguousarray(np.asarray(L0_ffn_router_w, dtype=f32))
    w1 = np.ascontiguousarray(np.asarray(L0_ffn_w1, dtype=f32))
    v1 = np.ascontiguousarray(np.asarray(L0_ffn_v1, dtype=f32))
    w2 = np.ascontiguousarray(np.asarray(L0_ffn_w2, dtype=f32))
    sgT = np.ascontiguousarray(np.asarray(L0_ffn_sg_w, dtype=f32).T)
    suT = np.ascontiguousarray(np.asarray(L0_ffn_su_w, dtype=f32).T)
    sdT = np.ascontiguousarray(np.asarray(L0_ffn_sd_w, dtype=f32).T)
    WrT = np.ascontiguousarray(router_w.T)
    bqkvT = np.ascontiguousarray(bqkv.reshape(48, 128).T)
    bqkvV = np.ascontiguousarray(bqkv[2 * D:3 * D].reshape(1, D))
    boT_np = np.ascontiguousarray(bo.reshape(16, 128).T)

    identf = np.eye(128, dtype=f32)
    ones_colb = np.ones((128, 1), dtype=ml_dtypes.bfloat16)
    ones_colf = np.ones((128, 1), dtype=f32)
    ones_row = np.ones((1, 512), dtype=f32)
    tri = np.zeros((128, 128), dtype=ml_dtypes.bfloat16)
    kk, qq = np.meshgrid(np.arange(128), np.arange(128), indexing="ij")
    tri[kk <= qq] = 1.0
    onehot = np.zeros((NEXP, NEXP * 128), dtype=ml_dtypes.bfloat16)
    for e in range(NEXP):
        onehot[e, e * 128:(e + 1) * 128] = 1.0

    shared = dict(Wqkv=Wqkv, bqkvT=bqkvT, bqkvV=bqkvV, Wo=Wo, boT=boT_np, WrT=WrT,
                  w1=w1, v1=v1, w2=w2, sgT=sgT, suT=suT, sdT=sdT,
                  identf=identf, ones_colb=ones_colb, ones_colf=ones_colf,
                  ones_row=ones_row, trimask=tri, onehot=onehot)
    in_maps = []
    for core in range(8):
        s, (a0, a1), (b0, b1) = _token_slices(core)
        xc = np.ascontiguousarray(np.concatenate([x[s, a0:a1], x[s, b0:b1]], axis=0))
        m = dict(shared)
        m["xin"] = xc
        in_maps.append(m)

    if _NC_CACHE[0] is None:
        _NC_CACHE[0] = build_nc()
    nc = _NC_CACHE[0]
    res = run_bass_kernel_spmd(nc, in_maps, core_ids=list(range(8)))

    out = np.empty((2, 4096, D), dtype=f32)
    for core in range(8):
        s, (a0, a1), (b0, b1) = _token_slices(core)
        yc = np.asarray(res.results[core]["y"])
        out[s, a0:a1] = yc[0:512]
        out[s, b0:b1] = yc[512:1024]
    if KV_DEBUG:
        global DBG_RESULTS
        DBG_RESULTS = res.results
    return out


# version 08
# speedup vs baseline: 1.0145x; 1.0144x over previous
import contextlib
import numpy as np
import ml_dtypes
import concourse.bass as bass
import concourse.mybir as mybir
from concourse.bass_utils import run_bass_kernel_spmd

F32 = mybir.dt.float32
BF16 = mybir.dt.bfloat16
AF = mybir.ActivationFunctionType
ALU = mybir.AluOpType
AX = mybir.AxisListType.X

D = 2048
NKC = 16
T = 1024
NH = 16
NEXP = 16
NVE = 18
FF = 1024
NFC = 8
EPS = 1e-5
ISQH = float(1.0 / np.sqrt(128.0))
NBIG = -1e30

ENGINES = ("sync", "tensor", "vector", "scalar", "gpsimd")

PHASES = 3
KV_DEBUG = False


class Plan:
    def __init__(self, nc, stack):
        self.nc = nc
        self.stack = stack
        self.ops = {e: [] for e in ENGINES}
        self.sems = {}
        self.cnt = {}
        self._redir = None

    def sem(self, name):
        self.sems[name] = self.stack.enter_context(self.nc.semaphore(name))
        self.cnt[name] = 0

    def op(self, eng, fn, waits=(), inc=None, amt=1):
        if inc is not None:
            self.cnt[inc] += amt
        entry = ["op", tuple(waits), fn, inc, amt, self.cnt[inc] if inc else 0]
        tgt = self._redir[eng] if self._redir is not None else self.ops[eng]
        tgt.append(entry)
        return self.cnt[inc] if inc else None

    def switch(self, n, engines, body_fn):
        assert self._redir is None
        base = dict(self.cnt)
        per_engine_bodies = {e: [] for e in engines}
        finals = []
        for b in range(n):
            self.cnt = dict(base)
            self._redir = {e: [] for e in engines}
            body_fn(b)
            for e in engines:
                per_engine_bodies[e].append(self._redir[e])
            self._redir = None
            finals.append(dict(self.cnt))
        for f in finals[1:]:
            assert f == finals[0], f"switch bodies unequal: {finals[0]} vs {f}"
        self.cnt = finals[0]
        for e in engines:
            self.ops[e].append(("switch", n, per_engine_bodies[e]))

    def barrier(self, extra=(), skip=()):
        waits = [(s, self.cnt[s]) for s in ("pe", "dve", "act")] + list(extra)
        for e in ENGINES:
            if e in skip:
                continue
            self.ops[e].append(["op", tuple(waits), None, None, 0, 0])

    def compress(self):
        import bisect
        csems = ("pe", "dve", "act")
        waited = {c: set() for c in csems}

        def scan(entries):
            for ent in entries:
                if ent[0] == "op":
                    for sname, v in ent[1]:
                        if sname in waited:
                            waited[sname].add(v)
                else:
                    for body in ent[2]:
                        scan(body)
        for e in ENGINES:
            scan(self.ops[e])
        keep = {c: sorted(waited[c] | {self.cnt[c]}) for c in csems}

        def rewrite(entries):
            for ent in entries:
                if ent[0] == "op":
                    ent[1] = tuple((sn, bisect.bisect_right(keep[sn], v) if sn in waited else v)
                                   for sn, v in ent[1])
                    if ent[3] in waited and ent[5] not in (set(keep[ent[3]]) if False else keep_sets[ent[3]]):
                        ent[3] = None
                else:
                    for body in ent[2]:
                        rewrite(body)
        keep_sets = {c: set(keep[c]) for c in csems}
        for e in ENGINES:
            rewrite(self.ops[e])
        for c in csems:
            self.cnt[c] = len(keep[c])

    def emit(self, block):
        plan = self

        def replay(eng, entries):
            for ent in entries:
                if ent[0] == "op":
                    _, waits, fn, inc, amt = ent[0], ent[1], ent[2], ent[3], ent[4]
                    for s, v in waits:
                        if v > 0:
                            eng.wait_ge(plan.sems[s], v)
                    if fn is not None:
                        ins = fn(eng)
                        if inc is not None:
                            ins.then_inc(plan.sems[inc], amt)
                else:
                    _, n, bodies = ent
                    cp = eng.partition_id() % n
                    for b in eng.Switch(cp, n):
                        replay(eng, bodies[b])

        def make(name):
            return lambda eng: replay(eng, plan.ops[name])

        block.sync(make("sync"))
        block.tensor(make("tensor"))
        block.vector(make("vector"))
        block.scalar(make("scalar"))
        block.gpsimd(make("gpsimd"))


def build_nc():
    nc = bass.Bass("TRN2", target_bir_lowering=False)

    dxin = nc.dram_tensor("xin", [T, D], F32, kind="ExternalInput")
    dWqkv = nc.dram_tensor("Wqkv", [D, 3 * D], F32, kind="ExternalInput")
    dbqkvT = nc.dram_tensor("bqkvT", [128, 48], F32, kind="ExternalInput")
    dbqkvV = nc.dram_tensor("bqkvV", [1, D], F32, kind="ExternalInput")
    dWo = nc.dram_tensor("Wo", [D, D], F32, kind="ExternalInput")
    dboT = nc.dram_tensor("boT", [128, 16], F32, kind="ExternalInput")
    dWrT = nc.dram_tensor("WrT", [D, NEXP], F32, kind="ExternalInput")
    dw1 = nc.dram_tensor("w1", [NEXP, D, FF], F32, kind="ExternalInput")
    dv1 = nc.dram_tensor("v1", [NEXP, D, FF], F32, kind="ExternalInput")
    dw2 = nc.dram_tensor("w2", [NEXP, FF, D], F32, kind="ExternalInput")
    dsgT = nc.dram_tensor("sgT", [D, D], F32, kind="ExternalInput")
    dsuT = nc.dram_tensor("suT", [D, D], F32, kind="ExternalInput")
    dsdT = nc.dram_tensor("sdT", [D, D], F32, kind="ExternalInput")
    didf = nc.dram_tensor("identf", [128, 128], F32, kind="ExternalInput")
    dones_colb = nc.dram_tensor("ones_colb", [128, 1], BF16, kind="ExternalInput")
    dones_colf = nc.dram_tensor("ones_colf", [128, 1], F32, kind="ExternalInput")
    dones_row = nc.dram_tensor("ones_row", [1, 512], F32, kind="ExternalInput")
    dtrimask = nc.dram_tensor("trimask", [128, 128], BF16, kind="ExternalInput")
    donehot = nc.dram_tensor("onehot", [NEXP, NEXP * 128], BF16, kind="ExternalInput")

    dy = nc.dram_tensor("y", [T, D], F32, kind="ExternalOutput")
    dKc = [nc.dram_tensor(f"Kc{i}", [D // 2, T], BF16, kind="Internal") for i in range(2)]
    dVc = [nc.dram_tensor(f"Vc{i}", [T, D // 2], BF16, kind="Internal") for i in range(2)]
    dKg = [nc.dram_tensor(f"Kg{i}", [4 * (D // 2), T], BF16, kind="Internal") for i in range(2)]
    dVg = [nc.dram_tensor(f"Vg{i}", [4 * T, D // 2], BF16, kind="Internal") for i in range(2)]

    st = contextlib.ExitStack()
    sb = lambda name, shape, dt: st.enter_context(nc.sbuf_tensor(name, shape, dt))
    resid = [sb(f"res{c}", [128, T], F32) for c in range(NKC)]
    xhat = [sb(f"xh{c}", [128, T], BF16) for c in range(NKC)]
    ones_row = sb("s_ones_row", [1, 512], F32)
    ones_colb = sb("s_ones_colb", [128, 1], BF16)
    ones_colf = sb("s_ones_colf", [128, 1], F32)
    identf = sb("s_identf", [128, 128], F32)
    trimask = sb("s_trimask", [128, 128], BF16)
    onehot = sb("s_onehot", [NEXP, NEXP * 128], BF16)
    wrT = sb("wrT", [128, NKC * NEXP], BF16)
    bqkvT = sb("s_bqkvT", [128, 48], F32)
    boT = sb("s_boT", [128, 16], F32)
    gsc = sb("gsc", [128, 80], F32)
    gT = sb("gT", [NEXP, T], BF16)
    ghat = sb("ghat", [128, 2 * T], F32)
    tscr = sb("tscr", [128, 1024], F32)
    arena = sb("arena", [128, 45056], BF16)

    pb = [st.enter_context(nc.psum_tensor(f"pb{i}", [128, 512], F32)) for i in range(8)]

    rowA = ghat[0:1, 0:T]
    rowB = ghat[0:1, T:2 * T]

    def A(off, n):
        return arena[:, off:off + n]

    s_w1 = A(0, 16384)
    s_v1 = A(16384, 16384)
    s_h = A(32768, 8192)
    s_w2 = [A(40960, 2048), A(43008, 2048)]
    s_wq = [A(0, 8192), A(8192, 8192)]
    s_Kst = [A(16384 + i * 1024, 1024) for i in range(4)]
    s_Vst = [A(20480 + i * 512, 512) for i in range(4)]
    s_Q = A(22528, 16384)
    s_Qh = lambda h: arena[:, 22528 + h * 1024:22528 + (h + 1) * 1024]
    s_Kc = [A(0, 4096), A(4096, 4096)]
    s_Vc = [A(8192, 4096), A(12288, 4096)]
    s_eS = [A(16384 + i * 512, 512) for i in range(4)]
    s_wo = [A(0, 8192), A(8192, 8192)]
    xstage = A(0, 32768).bitcast(F32)
    s_ost = [A(0, 4096).bitcast(F32), A(4096, 4096).bitcast(F32)]

    plan = Plan(nc, st)
    for s in ("pe", "dve", "act", "ws0", "ws1", "ws2", "w2s0", "w2s1",
              "xs", "kvs", "cc", "fs0", "fs1", "st0", "st1", "os",
              "kt0", "kt1", "kt2", "kt3", "vt0", "vt1", "vt2", "vt3"):
        plan.sem(s)
    P = plan.op

    state = dict(rnd=0, rel={0: [], 1: [], 2: [], 3: []})

    def round_begin():
        bs = state["rnd"] % 4
        waits = list(state["rel"][bs])
        state["rel"][bs] = []
        return bs, waits

    def round_end(release):
        state["rel"][state["rnd"] % 4] = list(release)
        state["rnd"] += 1

    def clear_rounds():
        for i in range(4):
            state["rel"][i] = []

    def dma(eng, fn, inc, waits=(), amt=16):
        return P(eng, fn, waits=waits, inc=inc, amt=amt)

    dma("sync", lambda e: e.dma_start(out=ones_row[:, :], in_=dones_row[:, :]), "xs")
    dma("sync", lambda e: e.dma_start(out=ones_colb[:, :], in_=dones_colb[:, :]), "xs")
    dma("sync", lambda e: e.dma_start(out=ones_colf[:, :], in_=dones_colf[:, :]), "xs")
    dma("sync", lambda e: e.dma_start(out=identf[:, :], in_=didf[:, :]), "xs")
    dma("sync", lambda e: e.dma_start(out=trimask[:, :], in_=dtrimask[:, :]), "xs")
    dma("sync", lambda e: e.dma_start(out=onehot[:, :], in_=donehot[:, :]), "xs")
    dma("sync", lambda e: e.dma_start(out=bqkvT[:, :], in_=dbqkvT[:, :]), "xs")
    dma("gpsimd", lambda e: e.dma_start(
        out=wrT[:, :].rearrange("p (k e) -> p k e", k=NKC),
        in_=dWrT.ap().rearrange("(k p) e -> p k e", p=128)), "kvs")
    kvs_wrt = plan.cnt["kvs"]
    dma("sync", lambda e: e.dma_start(
        out=xstage[:, 0:8192].rearrange("p (k f) -> p k f", k=4),
        in_=dxin.ap()[0:512, :].rearrange("(k p) f -> p k f", p=128)), "xs")
    xs_half = plan.cnt["xs"]
    dma("sync", lambda e: e.dma_start(
        out=xstage[:, 8192:16384].rearrange("p (k f) -> p k f", k=4),
        in_=dxin.ap()[512:1024, :].rearrange("(k p) f -> p k f", p=128)), "xs")
    xs_total = plan.cnt["xs"]

    def wsrc(mat, e, fh):
        c0, c1 = fh * 512, (fh + 1) * 512
        if e < NEXP:
            d = {"w1": dw1, "v1": dv1}[mat]
            return d.ap()[e][:, c0:c1].rearrange("(k p) f -> p k f", p=128)
        dd = {"w1": dsgT, "v1": dsuT}[mat]
        half = e - NEXP
        return dd.ap()[:, half * FF + c0:half * FF + c1].rearrange("(k p) f -> p k f", p=128)

    def w2src(e, j):
        if e < NEXP:
            return dw2.ap()[e][:, j * 256:(j + 1) * 256].rearrange("(k p) c -> p k c", p=128)
        half = e - NEXP
        return dsdT.ap()[half * FF:(half + 1) * FF, j * 256:(j + 1) * 256].rearrange(
            "(k p) c -> p k c", p=128)

    wready = {}
    w2ready = {}
    wsem_idx = [0]

    def issue_w1v1_half(e, fh, waits=()):
        s = f"ws{wsem_idx[0] % 3}"
        wsem_idx[0] += 1
        dst_w1 = s_w1.rearrange("p (k f) -> p k f", k=NKC)[:, :, fh * 512:(fh + 1) * 512]
        dst_v1 = s_v1.rearrange("p (k f) -> p k f", k=NKC)[:, :, fh * 512:(fh + 1) * 512]
        P("gpsimd", lambda eng, e=e, fh=fh, d=dst_w1: eng.dma_start(
            out=d, in_=wsrc("w1", e, fh)),
          waits=list(waits), inc=s, amt=16)
        v = P("gpsimd", lambda eng, e=e, fh=fh, d=dst_v1: eng.dma_start(
            out=d, in_=wsrc("v1", e, fh)),
          inc=s, amt=16)
        wready[(e, fh)] = (s, v)

    def issue_w1v1(e, waits=()):
        issue_w1v1_half(e, 0, waits=waits)
        issue_w1v1_half(e, 1, waits=waits)

    def issue_w2(e, j, waits=()):
        s = f"w2s{j % 2}"
        v = P("gpsimd", lambda eng, e=e, j=j: eng.dma_start(
            out=s_w2[j % 2].rearrange("p (k c) -> p k c", k=NFC), in_=w2src(e, j)),
          waits=list(waits), inc=s, amt=16)
        w2ready[(e, j)] = (s, v)

    gated = set()
    for tc in range(8):
        for half in range(2):
            bs, waits = round_begin()
            gate = xs_half if tc < 4 else xs_total
            if gate not in gated:
                waits = waits + [("xs", gate)]
                gated.add(gate)
            pe_end = None
            for j in range(8):
                c = half * 8 + j
                bank = pb[2 * bs + j // 4]
                i = j % 4
                pe_end = P("tensor", lambda e, tc=tc, c=c, bank=bank, i=i:
                           e.transpose(bank[:, i * 128:(i + 1) * 128],
                                       xstage[:, tc * 2048 + c * 128:tc * 2048 + (c + 1) * 128],
                                       identf[:, :]),
                           waits=waits if j == 0 else (), inc="pe")
            a_end = None
            for j in range(8):
                c = half * 8 + j
                bank = pb[2 * bs + j // 4]
                i = j % 4
                a_end = P("scalar", lambda e, tc=tc, c=c, bank=bank, i=i:
                          e.copy(resid[c][:, tc * 128:(tc + 1) * 128],
                                 bank[:, i * 128:(i + 1) * 128]),
                          waits=[("pe", pe_end)] if j == 0 else (), inc="act")
            round_end([("act", a_end)])
    plan.barrier()
    clear_rounds()

    def emit_ln():
        pe_sq = {}
        for c in range(NKC):
            for th in range(2):
                w = [("pe", pe_sq[(c - 1, th)])] if (c - 1, th) in pe_sq else []
                sq = P("scalar", lambda e, c=c, th=th:
                       e.activation(tscr[:, th * 512:(th + 1) * 512],
                                    resid[c][:, th * 512:(th + 1) * 512], AF.Square),
                       waits=w, inc="act")
                P("tensor", lambda e, c=c, th=th:
                  e.matmul(pb[th][0:1, :], ones_colf[:, 0:1],
                           resid[c][:, th * 512:(th + 1) * 512],
                           start=(c == 0), stop=(c == NKC - 1)))
                pe_sq[(c, th)] = P("tensor", lambda e, c=c, th=th:
                                   e.matmul(pb[2 + th][0:1, :], ones_colf[:, 0:1],
                                            tscr[:, th * 512:(th + 1) * 512],
                                            start=(c == 0), stop=(c == NKC - 1)),
                                   waits=[("act", sq)], inc="pe")
        pe_stats = plan.cnt["pe"]
        for th in range(2):
            a1 = P("scalar", lambda e, th=th:
                   e.activation(rowA[0:1, th * 512:(th + 1) * 512], pb[th][0:1, :],
                                AF.Copy, scale=1.0 / D),
                   waits=[("pe", pe_stats)], inc="act")
            d1 = P("vector", lambda e, th=th:
                   e.tensor_tensor(tscr[0:1, 0:512], rowA[0:1, th * 512:(th + 1) * 512],
                                   rowA[0:1, th * 512:(th + 1) * 512], ALU.mult),
                   waits=[("act", a1)], inc="dve")
            d2 = P("vector", lambda e, th=th:
                   e.scalar_tensor_tensor(tscr[32:33, 0:512], pb[2 + th][0:1, :],
                                          1.0 / D, tscr[0:1, 0:512], ALU.mult, ALU.subtract),
                   waits=[("dve", d1)], inc="dve")
            d3 = P("vector", lambda e:
                   e.tensor_scalar_add(tscr[0:1, 0:512], tscr[32:33, 0:512], EPS),
                   waits=[("dve", d2)], inc="dve")
            a2 = P("scalar", lambda e:
                   e.activation(tscr[32:33, 0:512], tscr[0:1, 0:512], AF.Sqrt),
                   waits=[("dve", d3)], inc="act")
            P("vector", lambda e, th=th:
              e.reciprocal(rowB[0:1, th * 512:(th + 1) * 512], tscr[32:33, 0:512]),
              waits=[("act", a2)], inc="dve")
        dve_stats = plan.cnt["dve"]
        pe_bc = None
        for th in range(2):
            P("tensor", lambda e, th=th:
              e.matmul(pb[4 + th][:, :], ones_row[0:1, 0:128],
                       rowA[0:1, th * 512:(th + 1) * 512], start=True, stop=True),
              waits=[("dve", dve_stats)] if th == 0 else (), inc="pe")
            pe_bc = P("tensor", lambda e, th=th:
                      e.matmul(pb[6 + th][:, :], ones_row[0:1, 0:128],
                               rowB[0:1, th * 512:(th + 1) * 512], start=True, stop=True),
                      inc="pe")
        prevm = 0
        xhat_ready = [0] * NKC
        for c in range(NKC):
            for th in range(2):
                sw = [("pe", pe_bc)] if c == 0 and th == 0 else []
                if prevm:
                    sw = sw + [("dve", prevm)]
                dsub = P("vector", lambda e, c=c, th=th:
                         e.tensor_tensor(tscr[:, 0:512], resid[c][:, th * 512:(th + 1) * 512],
                                         pb[4 + th][:, :], ALU.subtract),
                         waits=sw, inc="dve")
                prevm = P("vector", lambda e, c=c, th=th:
                          e.tensor_tensor(xhat[c][:, th * 512:(th + 1) * 512],
                                          tscr[:, 0:512], pb[6 + th][:, :], ALU.mult),
                          waits=[("dve", dsub)], inc="dve")
            xhat_ready[c] = prevm
        state["xhat_ready"] = xhat_ready
        plan.barrier(skip=("tensor",))
        clear_rounds()
        state["rel"][2] = [("dve", prevm)]
        state["rel"][3] = [("dve", prevm)]

    def emit_router():
        xr = state.get("xhat_ready", [0] * NKC)
        for ts in range(8):
            bank = pb[ts % 2]
            waits = [("kvs", kvs_wrt)] if ts == 0 else []
            if ts >= 2:
                waits = [("act", plan.cnt["act"])]
            pe_s = None
            for k in range(NKC):
                kw = list(waits) if k == 0 else []
                if ts == 0 and xr[k]:
                    kw = kw + [("dve", xr[k])]
                pe_s = P("tensor", lambda e, ts=ts, k=k, bank=bank:
                         e.matmul(bank[:, 0:16], xhat[k][:, ts * 128:(ts + 1) * 128],
                                  wrT[:, k * 16:(k + 1) * 16],
                                  start=(k == 0), stop=(k == NKC - 1)),
                         waits=kw, inc="pe")
            a0 = P("scalar", lambda e, bank=bank:
                   e.activation(gsc[:, 16:32], bank[:, 0:16], AF.Copy),
                   waits=[("pe", pe_s)], inc="act")
            d = P("vector", lambda e:
                  e.reduce_max(gsc[:, 64:65], gsc[:, 16:32], axis=AX, negate=True),
                  waits=[("act", a0)], inc="dve")
            a1 = P("scalar", lambda e:
                   e.activation(gsc[:, 0:16], gsc[:, 16:32], AF.Exp, bias=gsc[:, 64:65]),
                   waits=[("dve", d)], inc="act")
            cur, nxt = (0, 16)
            dd = P("vector", lambda e:
                   e.tensor_copy(gsc[:, 16:32], gsc[:, 0:16]),
                   waits=[("act", a1)], inc="dve")
            cur = 16
            nxt = 32
            for it in range(3):
                m = P("vector", lambda e, cur=cur:
                      e.reduce_max(gsc[:, 65:66], gsc[:, cur:cur + 16], axis=AX),
                      waits=[("dve", dd)], inc="dve")
                eq = P("vector", lambda e, cur=cur:
                       e.tensor_scalar(gsc[:, 48:64], gsc[:, cur:cur + 16],
                                       gsc[:, 65:66], None, ALU.is_ge),
                       waits=[("dve", m)], inc="dve")
                dd = P("vector", lambda e, cur=cur, nxt=nxt:
                       e.scalar_tensor_tensor(gsc[:, nxt:nxt + 16], gsc[:, 48:64],
                                              NBIG, gsc[:, cur:cur + 16], ALU.mult, ALU.add),
                       waits=[("dve", eq)], inc="dve")
                cur, nxt = nxt, cur
            tau = P("vector", lambda e, cur=cur:
                    e.reduce_max(gsc[:, 66:67], gsc[:, cur:cur + 16], axis=AX),
                    waits=[("dve", dd)], inc="dve")
            keep = P("vector", lambda e:
                     e.tensor_scalar(gsc[:, 48:64], gsc[:, 0:16], gsc[:, 66:67],
                                     None, ALU.is_ge),
                     waits=[("dve", tau)], inc="dve")
            gat = P("vector", lambda e:
                    e.tensor_tensor(gsc[:, 16:32], gsc[:, 0:16], gsc[:, 48:64], ALU.mult),
                    waits=[("dve", keep)], inc="dve")
            z = P("vector", lambda e:
                  e.reduce_sum(gsc[:, 67:68], gsc[:, 16:32], axis=AX),
                  waits=[("dve", gat)], inc="dve")
            r = P("vector", lambda e:
                  e.reciprocal(gsc[:, 68:69], gsc[:, 67:68]),
                  waits=[("dve", z)], inc="dve")
            gg = P("vector", lambda e:
                   e.tensor_scalar(gsc[:, 32:48], gsc[:, 16:32], gsc[:, 68:69],
                                   0.5, ALU.mult, ALU.mult),
                   waits=[("dve", r)], inc="dve")
            tp = P("tensor", lambda e, ts=ts, bank=pb[2 + ts % 2]:
                   e.transpose(bank[0:16, 0:128], gsc[:, 32:48], identf[:, :]),
                   waits=[("dve", gg)], inc="pe")
            P("scalar", lambda e, ts=ts, bank=pb[2 + ts % 2]:
              e.activation(gT[0:16, ts * 128:(ts + 1) * 128], bank[0:16, 0:128], AF.Copy),
              waits=[("pe", tp)], inc="act")
        plan.barrier()
        clear_rounds()

    def emit_expert(e):
        routed = e < NEXP
        if routed:
            bs, waits = round_begin()
            ge = []
            for th in range(2):
                m = P("tensor", lambda eng, e=e, th=th, bs=bs:
                      eng.matmul(pb[2 * bs + th][:, :], onehot[:, e * 128:(e + 1) * 128],
                                 gT[:, th * 512:(th + 1) * 512], start=True, stop=True),
                      waits=waits if th == 0 else (), inc="pe")
                g = P("scalar", lambda eng, th=th, bs=bs:
                      eng.activation(ghat[:, th * 512:(th + 1) * 512],
                                     pb[2 * bs + th][:, :], AF.Copy),
                      waits=[("pe", m)], inc="act")
                ge.append(g)
            round_end([("act", ge[-1])])
            ghat_ready = ge[-1]
        else:
            ghat_ready = None

        a_pe_end = None
        epi_dve_end = None
        half_pe_end = None
        scr_war = {0: 0, 1: 0}
        epi_f = [0] * NFC
        for f in range(NFC):
            for th in range(2):
                bs, waits = round_begin()
                if f == 0 and th == 0:
                    waits = waits + [wready[(e, 0)]]
                if f == 4 and th == 0:
                    waits = waits + [wready[(e, 1)]]
                ta = pb[2 * bs]
                tb = pb[2 * bs + 1]
                pe_end = None
                xr = state.get("xhat_ready", [0] * NKC) if (e == 0 and f == 0 and th == 0) else None
                for k in range(NKC):
                    kw = list(waits) if k == 0 else []
                    if xr is not None and xr[k]:
                        kw = kw + [("dve", xr[k])]
                    for mat, t in ((s_w1, ta), (s_v1, tb)):
                        pe_end = P("tensor", lambda eng, k=k, f=f, mat=mat, th=th, t=t:
                                   eng.matmul(t[:, :], mat[:, k * FF + f * 128:k * FF + (f + 1) * 128],
                                              xhat[k][:, th * 512:(th + 1) * 512],
                                              start=(k == 0), stop=(k == NKC - 1)),
                                   waits=kw if t is ta else (), inc="pe")
                a_pe_end = pe_end
                if f == 3 and th == 1:
                    half_pe_end = pe_end
                scrA = tscr[:, 0:512] if th == 0 else ghat[:, 1024:1536]
                scrB = tscr[:, 512:1024] if th == 0 else ghat[:, 1536:2048]
                sl = P("scalar", lambda eng, t=ta, scrA=scrA:
                       eng.activation(scrA, t[:, :], AF.Silu),
                       waits=[("pe", pe_end), ("dve", scr_war[th])], inc="act")
                if routed:
                    gw = [("act", sl)]
                    if ghat_ready is not None and f == 0 and th == 0:
                        gw.append(("act", ghat_ready))
                    d1 = P("vector", lambda eng, th=th, scrA=scrA, scrB=scrB:
                          eng.tensor_tensor(scrB, scrA,
                                            ghat[:, th * 512:(th + 1) * 512], ALU.mult),
                          waits=gw, inc="dve")
                    d2 = P("vector", lambda eng, f=f, th=th, t=tb, scrB=scrB:
                           eng.tensor_tensor(s_h[:, f * FF + th * 512:f * FF + (th + 1) * 512],
                                             scrB, t[:, :], ALU.mult),
                           waits=[("dve", d1)], inc="dve")
                    scr_war[th] = d2
                else:
                    d2 = P("vector", lambda eng, f=f, th=th, t=tb, scrA=scrA:
                           eng.scalar_tensor_tensor(
                               s_h[:, f * FF + th * 512:f * FF + (th + 1) * 512],
                               scrA, 0.5, t[:, :], ALU.mult, ALU.mult),
                           waits=[("act", sl)], inc="dve")
                    scr_war[th] = d2
                epi_dve_end = d2
                if th == 1:
                    epi_f[f] = d2
                round_end([("dve", d2), ("act", sl)])
        if e + 1 < NVE:
            issue_w1v1_half(e + 1, 0, waits=[("pe", half_pe_end)])
            issue_w1v1_half(e + 1, 1, waits=[("pe", a_pe_end)])
        for j in range(NFC):
            first_of_j = True
            for cc in range(2):
                bs, waits = round_begin()
                if first_of_j:
                    s2, v2 = w2ready[(e, j)]
                    waits = waits + [(s2, v2)]
                    first_of_j = False
                tiles = [pb[2 * bs], pb[2 * bs + 1]]
                pe_end = None
                for k in range(NFC):
                    kw = list(waits) if (k == 0) else []
                    if j == 0 and cc == 0:
                        kw = kw + [("dve", epi_f[k])]
                    for th in range(2):
                        pe_end = P("tensor", lambda eng, k=k, j=j, cc=cc, th=th, t=tiles[th]:
                                   eng.matmul(t[:, :],
                                              s_w2[j % 2][:, k * 256 + cc * 128:k * 256 + (cc + 1) * 128],
                                              s_h[:, k * FF + th * 512:k * FF + (th + 1) * 512],
                                              start=(k == 0), stop=(k == NFC - 1)),
                                   waits=kw if th == 0 else (), inc="pe")
                if cc == 1:
                    nj = j + 2
                    ne = e
                    if nj >= NFC:
                        ne, nj = e + 1, nj - NFC
                    if ne < NVE:
                        issue_w2(ne, nj, waits=[("pe", pe_end)])
                dl = None
                for th in range(2):
                    cg = 2 * j + cc
                    dl = P("vector", lambda eng, cg=cg, th=th, t=tiles[th]:
                           eng.tensor_tensor(resid[cg][:, th * 512:(th + 1) * 512],
                                             resid[cg][:, th * 512:(th + 1) * 512],
                                             t[:, :], ALU.add),
                           waits=[("pe", pe_end)] if th == 0 else (), inc="dve")
                round_end([("dve", dl)])

    def emit_ffn():
        issue_w2(0, 0)
        issue_w2(0, 1)
        for e in range(NVE):
            emit_expert(e)
        plan.barrier()
        clear_rounds()

    qkv_pe_gate = [0]

    def emit_qkv():
        dma("sync", lambda e: e.dma_start(out=ghat[0:1, 0:1024], in_=dbqkvV[0:1, 0:1024]), "fs0")
        dma("sync", lambda e: e.dma_start(out=ghat[0:1, 1024:2048], in_=dbqkvV[0:1, 1024:2048]), "fs0")
        fs0_bias = plan.cnt["fs0"]

        wq_pe_end = {}

        def issue_wq(g, waits=()):
            s = f"ws{wsem_idx[0] % 3}"
            wsem_idx[0] += 1
            v = P("gpsimd", lambda eng, g=g: eng.dma_start(
                out=s_wq[g % 2].rearrange("p (k c) -> p k c", k=NKC),
                in_=dWqkv.ap()[:, g * 512:(g + 1) * 512].rearrange("(k p) c -> p k c", p=128)),
                waits=list(waits), inc=s, amt=16)
            return (s, v)

        xr = state.get("xhat_ready", [0] * NKC)
        gseq = [4, 5, 6, 7, 8, 9, 10, 11, 0, 1, 2, 3]
        wqr = {gseq[0]: issue_wq(gseq[0], waits=[("pe", qkv_pe_gate[0])]),
               gseq[1]: issue_wq(gseq[1])}
        kst = [0]
        vst = [0]
        kslot_store = {}
        vslot_store = {}
        for gi, g in enumerate(gseq):
            swq, vwq = wqr[g]
            if g < 8:
                for f in range(4):
                    bs, waits = round_begin()
                    waits = waits + ([(swq, vwq)] if f == 0 else [])
                    pe_end = None
                    tiles = [pb[2 * bs], pb[2 * bs + 1]]
                    for k in range(NKC):
                        kw = list(waits) if (k == 0) else []
                        if gi == 0 and f == 0 and xr[k]:
                            kw = kw + [("dve", xr[k])]
                        for th in range(2):
                            pe_end = P("tensor", lambda eng, g=g, k=k, f=f, th=th, t=tiles[th]:
                                       eng.matmul(t[:, :],
                                                  s_wq[g % 2][:, k * 512 + f * 128:k * 512 + (f + 1) * 128],
                                                  xhat[k][:, th * 512:(th + 1) * 512],
                                                  start=(k == 0), stop=(k == NKC - 1)),
                                       waits=kw if th == 0 else (), inc="pe")
                    if f == 3 and gi + 2 < 12:
                        wqr[gseq[gi + 2]] = issue_wq(gseq[gi + 2], waits=[("pe", pe_end)])
                    al = None
                    fk = g * 4 + f
                    ks = kst[0] % 4
                    for th in range(2):
                        if fk < 16:
                            dst = s_Qh(fk)
                            al = P("vector", lambda eng, dst=dst, th=th, fk=fk, t=tiles[th]:
                                   eng.tensor_scalar(dst[:, th * 512:(th + 1) * 512], t[:, :],
                                                     bqkvT[:, fk:fk + 1], None, ALU.add),
                                   waits=[("pe", pe_end)] if th == 0 else (), inc="dve")
                        else:
                            kw = [("pe", pe_end)] if th == 0 else []
                            if th == 0 and ks in kslot_store:
                                kw = kw + [(f"kt{ks}", kslot_store[ks])]
                            al = P("vector", lambda eng, ks=ks, th=th, fk=fk, t=tiles[th]:
                                   eng.tensor_scalar(s_Kst[ks][:, th * 512:(th + 1) * 512], t[:, :],
                                                     bqkvT[:, fk:fk + 1], None, ALU.add),
                                   waits=kw, inc="dve")
                            if th == 1:
                                fc = fk - 16
                                half, fcl = (0, fc) if fc < 8 else (1, fc - 8)
                                kslot_store[ks] = dma("sync", lambda eng, ks=ks, half=half, fcl=fcl:
                                    eng.dma_start(
                                        out=dKc[half].ap()[fcl * 128:(fcl + 1) * 128, :],
                                        in_=s_Kst[ks][:, :]),
                                    f"kt{ks}", waits=[("dve", al)])
                                kst[0] += 1
                    round_end([("dve", al)])
            else:
                hg = g - 8
                bs, waits = round_begin()
                brow = rowA if hg < 2 else rowB
                boff = (hg % 2) * 512
                pe_b = P("tensor", lambda e, t=pb[2 * bs], brow=brow, boff=boff:
                         e.matmul(t[:, :], ones_row[0:1, 0:128],
                                  brow[0:1, boff:boff + 512], start=True, stop=True),
                         waits=waits + [("fs0", fs0_bias), ("dve", plan.cnt["dve"])], inc="pe")
                a_b = P("vector", lambda e, t=pb[2 * bs]:
                        e.tensor_copy(tscr[:, 512:1024], t[:, :]),
                        waits=[("pe", pe_b)], inc="dve")
                round_end([("dve", a_b)])
                vbias_ready = a_b
                for r in range(4):
                    bs, waits = round_begin()
                    waits = waits + ([(swq, vwq)] if r == 0 else [])
                    tiles = [pb[2 * bs], pb[2 * bs + 1]]
                    pe_end = None
                    for i in range(2):
                        ts = 2 * r + i
                        for k in range(NKC):
                            pe_end = P("tensor", lambda eng, g=g, k=k, ts=ts, t=tiles[i]:
                                       eng.matmul(t[:, :],
                                                  xhat[k][:, ts * 128:(ts + 1) * 128],
                                                  s_wq[g % 2][:, k * 512:(k + 1) * 512],
                                                  start=(k == 0), stop=(k == NKC - 1)),
                                       waits=waits if (k == 0 and i == 0) else (), inc="pe")
                    if r == 3 and gi + 2 < 12:
                        wqr[gseq[gi + 2]] = issue_wq(gseq[gi + 2], waits=[("pe", pe_end)])
                    al = None
                    for i in range(2):
                        ts = 2 * r + i
                        vs = vst[0] % 4
                        vw = [("pe", pe_end)] if i == 0 else []
                        if i == 0 and r == 0:
                            vw = vw + [("dve", vbias_ready)]
                        if vs in vslot_store:
                            vw = vw + [(f"vt{vs}", vslot_store[vs])]
                        al = P("vector", lambda eng, vs=vs, t=tiles[i]:
                               eng.tensor_tensor(s_Vst[vs][:, :], t[:, :],
                                                 tscr[:, 512:1024], ALU.add),
                               waits=vw, inc="dve")
                        vh, hgl = (0, hg) if hg < 2 else (1, hg - 2)
                        vslot_store[vs] = dma("sync", lambda eng, vs=vs, ts=ts, vh=vh, hgl=hgl:
                            eng.dma_start(
                                out=dVc[vh].ap()[ts * 128:(ts + 1) * 128, hgl * 512:(hgl + 1) * 512],
                                in_=s_Vst[vs][:, :]),
                            f"vt{vs}", waits=[("dve", al)])
                        vst[0] += 1
                    round_end([("dve", al)])
            if g in (5, 7):
                kh = 0 if g == 5 else 1
                P("gpsimd", lambda e, kh=kh: e.collective_compute(
                    "AllGather", ALU.bypass, replica_groups=[[0, 1, 2, 3], [4, 5, 6, 7]],
                    ins=[dKc[kh].ap().opt()], outs=[dKg[kh].ap().opt()]),
                  waits=[(f"kt{i}", plan.cnt[f"kt{i}"]) for i in range(4)], inc="cc", amt=1)
            if g in (9, 11):
                vh = 0 if g == 9 else 1
                P("gpsimd", lambda e, vh=vh: e.collective_compute(
                    "AllGather", ALU.bypass, replica_groups=[[0, 1, 2, 3], [4, 5, 6, 7]],
                    ins=[dVc[vh].ap().opt()], outs=[dVg[vh].ap().opt()]),
                  waits=[(f"vt{i}", plan.cnt[f"vt{i}"]) for i in range(4)], inc="cc", amt=1)
        plan.barrier()
        clear_rounds()

    def emit_flash():
        cc_ka, cc_kb, cc_va, cc_vb = (plan.cnt["cc"] - 3, plan.cnt["cc"] - 2,
                                      plan.cnt["cc"] - 1, plan.cnt["cc"])

        def body(cp):
            qlo, qhi = cp, 7 - cp
            pe_head_end = {}

            def issue_kv(h, extra_waits):
                s = f"fs{h % 2}"
                kh = 0 if h < 8 else 1
                hl = h % 8
                cck = cc_ka if kh == 0 else cc_kb
                ccv = cc_va if kh == 0 else cc_vb
                first = (h % 8) < 2
                for kb in range(8):
                    owner = kb if kb < 4 else 7 - kb
                    slot = 0 if kb < 4 else 1
                    w = list(extra_waits) if kb == 0 else []
                    if kb == 0 and first:
                        w = w + [("cc", cck)]
                    P("sync", lambda eng, h=h, kb=kb, owner=owner, slot=slot, kh=kh, hl=hl:
                      eng.dma_start(
                          out=s_Kc[h % 2][:, kb * 512:(kb + 1) * 512],
                          in_=dKg[kh].ap()[owner * 1024 + hl * 128:owner * 1024 + (hl + 1) * 128,
                                           slot * 512:(slot + 1) * 512]),
                      waits=w, inc=s, amt=16)
                last = None
                for kb in range(8):
                    owner = kb if kb < 4 else 7 - kb
                    slot = 0 if kb < 4 else 1
                    w = [("cc", ccv)] if (kb == 0 and first) else []
                    last = P("sync", lambda eng, h=h, kb=kb, owner=owner, slot=slot, kh=kh, hl=hl:
                             eng.dma_start(
                                 out=s_Vc[h % 2][:, kb * 4 * 128:(kb + 1) * 4 * 128]
                                 .rearrange("p (s d) -> p s d", s=4),
                                 in_=dVg[kh].ap()[owner * T + slot * 512:owner * T + slot * 512 + 512,
                                                  hl * 128:(hl + 1) * 128]
                                 .rearrange("(s p) d -> p s d", p=128)),
                             waits=w, inc=s, amt=16)
                return last

            sbank = [pb[0], pb[1], pb[3], pb[7]]
            fsv = {}
            fsv[0] = issue_kv(0, [])
            fsv[1] = issue_kv(1, [])
            act_eS = {}
            pe_eS = {}
            prev_norm_dve = {0: 0, 1: 0}
            prev_l_act = 0
            prev_rb_act = 0
            for h in range(NH):
                fwait = (f"fs{h % 2}", fsv[h])
                first_of_head = True
                for si, qb in enumerate((qlo, qhi)):
                    kts = 4 * (qb + 1)
                    avu = pb[4 + si]
                    lps = pb[2]

                    def emit_sc(kt, fw):
                        qrel = max(0, kt * 128 - qb * 512)
                        w = 512 - qrel
                        sw = list(fw)
                        if kt % 4 in act_eS:
                            sw.append(("act", act_eS[kt % 4]))
                        return P("tensor", lambda eng, h=h, kt=kt, si=si, qrel=qrel, w=w, sb=sbank:
                                 eng.matmul(sb[kt % 4][:, 0:w],
                                            s_Kc[h % 2][:, kt * 128:(kt + 1) * 128],
                                            s_Qh(h)[:, si * 512 + qrel:si * 512 + 512],
                                            start=True, stop=True),
                                 waits=sw, inc="pe")

                    sc_of = {}
                    for kt in range(4):
                        if kt < kts:
                            fw = [fwait] if (kt == 0 and first_of_head) else []
                            sc_of[kt] = emit_sc(kt, fw)
                    first_of_head = False
                    for kt in range(kts):
                        qrel = max(0, kt * 128 - qb * 512)
                        w = 512 - qrel
                        diag = kt >= qb * 4
                        es = s_eS[kt % 4]
                        ew = [("pe", sc_of[kt])]
                        if kt % 4 in pe_eS:
                            ew.append(("pe", max(sc_of[kt], pe_eS[kt % 4])))
                        ae = P("scalar", lambda eng, kt=kt, w=w, es=es, sb=sbank:
                               eng.activation(es[:, 0:w], sb[kt % 4][:, 0:w],
                                              AF.Exp, scale=ISQH),
                               waits=ew, inc="act")
                        act_eS[kt % 4] = ae
                        if diag:
                            md = P("vector", lambda eng, es=es:
                                   eng.tensor_tensor(es[:, 0:128], es[:, 0:128],
                                                     trimask[:, :], ALU.mult),
                                   waits=[("act", ae)], inc="dve")
                            mmw = [("dve", md)]
                        else:
                            mmw = [("act", ae)]
                        if kt == 0:
                            mmw = mmw + [("dve", prev_norm_dve[si])]
                        P("tensor", lambda eng, h=h, kt=kt, qrel=qrel, w=w, es=es, avu=avu, kts=kts:
                          eng.matmul(avu[:, qrel:512],
                                     s_Vc[h % 2][:, kt * 128:(kt + 1) * 128],
                                     es[:, 0:w], start=(kt == 0), stop=(kt == kts - 1)),
                          waits=mmw, inc="pe")
                        lw = [("dve", prev_l_act)] if kt == 0 else []
                        lm = P("tensor", lambda eng, kt=kt, qrel=qrel, w=w, es=es, lps=lps, kts=kts:
                               eng.matmul(lps[0:1, qrel:512], ones_colb[:, 0:1],
                                          es[:, 0:w], start=(kt == 0), stop=(kt == kts - 1)),
                               waits=lw, inc="pe")
                        pe_eS[kt % 4] = lm
                        if kt + 4 < kts:
                            sc_of[kt + 4] = emit_sc(kt + 4, [])
                    la = P("vector", lambda eng, lps=lps:
                           eng.tensor_copy(tscr[0:1, 0:512], lps[0:1, 0:512]),
                           waits=[("pe", plan.cnt["pe"])], inc="dve")
                    prev_l_act = la
                    rc = P("vector", lambda eng:
                           eng.reciprocal(tscr[0:1, 512:1024], tscr[0:1, 0:512]),
                           waits=[("dve", la)], inc="dve")
                    rb = P("tensor", lambda eng:
                           eng.matmul(pb[6][:, :], ones_row[0:1, 0:128],
                                      tscr[0:1, 512:1024], start=True, stop=True),
                           waits=[("dve", rc), ("dve", prev_rb_act)], inc="pe")
                    ra = P("vector", lambda eng:
                           eng.tensor_copy(tscr[:, 512:1024], pb[6][:, :]),
                           waits=[("pe", rb)], inc="dve")
                    prev_rb_act = ra
                    nd = P("vector", lambda eng, h=h, si=si, avu=avu:
                           eng.tensor_tensor(xhat[h][:, si * 512:(si + 1) * 512],
                                             avu[:, :], tscr[:, 512:1024], ALU.mult),
                           waits=[("dve", ra)], inc="dve")
                    prev_norm_dve[si] = nd
                pe_head_end[h] = plan.cnt["pe"]
                if h + 2 < NH:
                    fsv[h + 2] = issue_kv(h + 2, [("pe", pe_head_end[h])])

        plan.switch(4, ["sync", "tensor", "vector", "scalar"], body)
        plan.barrier()
        clear_rounds()

    def emit_wo():
        dma("sync", lambda e: e.dma_start(out=boT[:, :], in_=dboT[:, :]), "fs0")
        fs0_bo = plan.cnt["fs0"]

        wor = {}

        def issue_wo(cg, waits=()):
            s = f"ws{wsem_idx[0] % 3}"
            wsem_idx[0] += 1
            v = P("gpsimd", lambda eng, cg=cg: eng.dma_start(
                out=s_wo[cg % 2].rearrange("p (k c) -> p k c", k=NKC),
                in_=dWo.ap()[:, cg * 512:(cg + 1) * 512].rearrange("(k p) c -> p k c", p=128)),
                waits=list(waits), inc=s, amt=16)
            return (s, v)

        pe_now = plan.cnt["pe"]
        wor[0] = issue_wo(0, waits=[("pe", pe_now)])
        wor[1] = issue_wo(1)
        for cg in range(4):
            swo, vwo = wor[cg]
            for cc in range(4):
                bs, waits = round_begin()
                if cc == 0:
                    waits = waits + [(swo, vwo), ("fs0", fs0_bo)]
                tiles = [pb[2 * bs], pb[2 * bs + 1]]
                pe_end = None
                for k in range(NKC):
                    for th in range(2):
                        pe_end = P("tensor", lambda eng, cg=cg, k=k, cc=cc, th=th, t=tiles[th]:
                                   eng.matmul(t[:, :],
                                              s_wo[cg % 2][:, k * 512 + cc * 128:k * 512 + (cc + 1) * 128],
                                              xhat[k][:, th * 512:(th + 1) * 512],
                                              start=(k == 0), stop=(k == NKC - 1)),
                                   waits=waits if (k == 0 and th == 0) else (), inc="pe")
                if cc == 3 and cg + 2 < 4:
                    wor[cg + 2] = issue_wo(cg + 2, waits=[("pe", pe_end)])
                dl = None
                cglob = cg * 4 + cc
                for th in range(2):
                    dl = P("vector", lambda eng, cglob=cglob, th=th, t=tiles[th]:
                           eng.scalar_tensor_tensor(resid[cglob][:, th * 512:(th + 1) * 512],
                                                    t[:, :], boT[:, cglob:cglob + 1],
                                                    resid[cglob][:, th * 512:(th + 1) * 512],
                                                    ALU.add, ALU.add),
                           waits=[("pe", pe_end)] if th == 0 else (), inc="dve")
                round_end([("dve", dl)])
        plan.barrier()
        clear_rounds()

    def emit_out():
        for tc in range(8):
            a_end = None
            for half in range(2):
                bs, waits = round_begin()
                pe_end = None
                for j in range(8):
                    c = half * 8 + j
                    bank = pb[2 * bs + j // 4]
                    i = j % 4
                    pe_end = P("tensor", lambda e, tc=tc, c=c, bank=bank, i=i:
                               e.transpose(bank[:, i * 128:(i + 1) * 128],
                                           resid[c][:, tc * 128:(tc + 1) * 128],
                                           identf[:, :]),
                               waits=waits if j == 0 else (), inc="pe")
                for j in range(8):
                    c = half * 8 + j
                    bank = pb[2 * bs + j // 4]
                    i = j % 4
                    aw = [("pe", pe_end)] if j == 0 else []
                    if half == 0 and j == 0:
                        aw = aw + [(f"st{tc % 2}", plan.cnt[f"st{tc % 2}"])]
                    a_end = P("scalar", lambda e, tc=tc, c=c, bank=bank, i=i:
                              e.copy(s_ost[tc % 2][:, c * 128:(c + 1) * 128],
                                     bank[:, i * 128:(i + 1) * 128]),
                              waits=aw, inc="act")
                round_end([("act", a_end)])
            dma("sync", lambda e, tc=tc: e.dma_start(
                out=dy.ap()[tc * 128:(tc + 1) * 128, :], in_=s_ost[tc % 2][:, :]),
                f"st{tc % 2}", waits=[("act", a_end)])
        fin = (("st0", plan.cnt["st0"]), ("st1", plan.cnt["st1"]))
        for e in ENGINES:
            plan.ops[e].append(["op", fin, None, None, 0, 0])

    issue_w1v1(0, waits=[("pe", plan.cnt["pe"])])
    emit_ln()
    emit_router()
    emit_ffn()
    if PHASES >= 2:
        qkv_pe_gate[0] = plan.cnt["pe"]
        emit_ln()
        emit_qkv()
        emit_flash()
        emit_wo()
    if PHASES >= 3:
        issue_w1v1(0, waits=[("pe", plan.cnt["pe"])])
        emit_ln()
        emit_router()
        emit_ffn()
    emit_out()

    with nc.Block() as block, st:
        plan.emit(block)
    return nc


_NC_CACHE = [None]


def _token_slices(core):
    cp = core % 4
    s = core // 4
    return s, (cp * 512, (cp + 1) * 512), ((7 - cp) * 512, (8 - cp) * 512)


def kernel(L0_attn_x, L0_attn_Wqkv, L0_attn_bqkv, L0_attn_Wo, L0_attn_bo,
           L0_ffn_router_w, L0_ffn_w1, L0_ffn_v1, L0_ffn_w2,
           L0_ffn_sg_w, L0_ffn_su_w, L0_ffn_sd_w):
    f32 = np.float32
    x = np.ascontiguousarray(np.asarray(L0_attn_x, dtype=f32))
    Wqkv = np.ascontiguousarray(np.asarray(L0_attn_Wqkv, dtype=f32))
    bqkv = np.ascontiguousarray(np.asarray(L0_attn_bqkv, dtype=f32))
    Wo = np.ascontiguousarray(np.asarray(L0_attn_Wo, dtype=f32))
    bo = np.ascontiguousarray(np.asarray(L0_attn_bo, dtype=f32))
    router_w = np.ascontiguousarray(np.asarray(L0_ffn_router_w, dtype=f32))
    w1 = np.ascontiguousarray(np.asarray(L0_ffn_w1, dtype=f32))
    v1 = np.ascontiguousarray(np.asarray(L0_ffn_v1, dtype=f32))
    w2 = np.ascontiguousarray(np.asarray(L0_ffn_w2, dtype=f32))
    sgT = np.ascontiguousarray(np.asarray(L0_ffn_sg_w, dtype=f32).T)
    suT = np.ascontiguousarray(np.asarray(L0_ffn_su_w, dtype=f32).T)
    sdT = np.ascontiguousarray(np.asarray(L0_ffn_sd_w, dtype=f32).T)
    WrT = np.ascontiguousarray(router_w.T)
    bqkvT = np.ascontiguousarray(bqkv.reshape(48, 128).T)
    bqkvV = np.ascontiguousarray(bqkv[2 * D:3 * D].reshape(1, D))
    boT_np = np.ascontiguousarray(bo.reshape(16, 128).T)

    identf = np.eye(128, dtype=f32)
    ones_colb = np.ones((128, 1), dtype=ml_dtypes.bfloat16)
    ones_colf = np.ones((128, 1), dtype=f32)
    ones_row = np.ones((1, 512), dtype=f32)
    tri = np.zeros((128, 128), dtype=ml_dtypes.bfloat16)
    kk, qq = np.meshgrid(np.arange(128), np.arange(128), indexing="ij")
    tri[kk <= qq] = 1.0
    onehot = np.zeros((NEXP, NEXP * 128), dtype=ml_dtypes.bfloat16)
    for e in range(NEXP):
        onehot[e, e * 128:(e + 1) * 128] = 1.0

    shared = dict(Wqkv=Wqkv, bqkvT=bqkvT, bqkvV=bqkvV, Wo=Wo, boT=boT_np, WrT=WrT,
                  w1=w1, v1=v1, w2=w2, sgT=sgT, suT=suT, sdT=sdT,
                  identf=identf, ones_colb=ones_colb, ones_colf=ones_colf,
                  ones_row=ones_row, trimask=tri, onehot=onehot)
    in_maps = []
    for core in range(8):
        s, (a0, a1), (b0, b1) = _token_slices(core)
        xc = np.ascontiguousarray(np.concatenate([x[s, a0:a1], x[s, b0:b1]], axis=0))
        m = dict(shared)
        m["xin"] = xc
        in_maps.append(m)

    if _NC_CACHE[0] is None:
        _NC_CACHE[0] = build_nc()
    nc = _NC_CACHE[0]
    res = run_bass_kernel_spmd(nc, in_maps, core_ids=list(range(8)))

    out = np.empty((2, 4096, D), dtype=f32)
    for core in range(8):
        s, (a0, a1), (b0, b1) = _token_slices(core)
        yc = np.asarray(res.results[core]["y"])
        out[s, a0:a1] = yc[0:512]
        out[s, b0:b1] = yc[512:1024]
    if KV_DEBUG:
        global DBG_RESULTS
        DBG_RESULTS = res.results
    return out
